# version 04
# baseline (speedup 1.0000x reference)
import contextlib
import numpy as np

import concourse.bass as bass
import concourse.mybir as mybir
from concourse.bass_utils import run_bass_kernel_spmd

F32 = mybir.dt.float32
BF16 = mybir.dt.bfloat16
AF = mybir.ActivationFunctionType
ALU = mybir.AluOpType
AX = mybir.AxisListType

ENGS = ("sync", "tensor", "vector", "scalar", "gpsimd")

D = 1024
H = 16
HF = 2048
NE = 9
NCH = D // 128


class Cfg:
    def __init__(self, SB=16384, OWN=4096, debug=False):
        self.SB = SB
        self.OWN = OWN
        self.debug = debug
        self.nT = SB // 128
        self.NQ = max(1, OWN // 1024)
        self.Q = OWN // self.NQ
        self.tbs = self.Q // 128
        self.sgs = max(1, self.Q // 512)


class Arena:

    def __init__(self, nc, es, name, nbytes):
        self.nc = nc
        self.h = es.enter_context(
            nc.sbuf_tensor(name, [128, nbytes], mybir.dt.uint8))
        self.base = nc.lookup_mloc(self.h).addr
        self.nbytes = nbytes
        self.cur = {}

    def at(self, tag, name, shape, dt):
        per = 1
        for s in shape[1:]:
            per *= s
        per *= mybir.dt.size(dt)
        off = (self.cur.get(tag, 0) + 63) & ~63
        self.cur[tag] = off + per
        assert self.cur[tag] <= self.nbytes, (name, tag, self.cur[tag])
        return self.nc.alloc_sbuf_tensor_at(name, shape, dt,
                                            offset=self.base + off)


class Prog:
    def __init__(self, nc, es):
        self.nc = nc
        self.es = es
        self.q = {e: [] for e in ENGS}
        self.sems = {}
        self.cnt = {}
        self.last_wait = {e: {} for e in ENGS}
        self.use = {}
        self.wrt = {}
        self.pend = {e: [] for e in ENGS}

    def sem(self, name):
        if name not in self.sems:
            self.sems[name] = self.es.enter_context(self.nc.semaphore(name))
            self.cnt[name] = 0
        return name

    def _raw(self, eng, fn, waits, inc, k):
        ws = []
        for s, v in waits:
            if v is None:
                continue
            assert v <= self.cnt.get(s, 0), (s, v, self.cnt.get(s, 0))
            if self.last_wait[eng].get(s, -1) < v:
                ws.append((s, v))
                self.last_wait[eng][s] = v
        mark = None
        if inc is not None:
            self.sem(inc)
            self.cnt[inc] += k
            mark = self.cnt[inc]
        self.q[eng].append((fn, tuple(ws), inc, k))
        return mark

    def op(self, eng, fn, reads=(), writes=(), extra=(), inc=None, k=1,
           wguard=True):
        waits = []
        if wguard:
            for b in writes:
                waits += list(self.use.get(b, {}).items())
                waits += list(self.wrt.get(b, {}).items())
        for b in reads:
            waits += list(self.wrt.get(b, {}).items())
        for x in extra:
            if x is not None:
                waits.append(x)
        mark = self._raw(eng, fn, waits, inc, k)
        for b in writes:
            self.pend[eng].append(("w", b))
        for b in reads:
            self.pend[eng].append(("r", b))
        if inc is not None:
            for kind, b in self.pend[eng]:
                if kind == "w":
                    self.use[b] = {}
                    self.wrt[b] = {inc: mark}
                else:
                    u = self.use.setdefault(b, {})
                    u[inc] = mark
            self.pend[eng] = []
        return mark

    def drain(self, eng, inc=None):
        return self.op(eng, lambda E: E.drain(), inc=inc)

    def replay(self, block):
        for eng in ENGS:
            deco = getattr(block, eng)

            def make(eng):
                steps = self.q[eng]
                sems = self.sems

                def run(E, steps=steps, sems=sems):
                    for fn, ws, inc, k in steps:
                        for s, v in ws:
                            E.wait_ge(sems[s], v)
                        inst = fn(E)
                        if inc is not None:
                            inst.then_inc(sems[inc], k)
                return run
            deco(make(eng))


def build_nc(cfg: Cfg):
    nc = bass.Bass("TRN2", target_bir_lowering=False)
    SB, OWN = cfg.SB, cfg.OWN

    d_xb = nc.dram_tensor("xb", [SB, D], F32, kind="ExternalInput").ap()
    d_xo = nc.dram_tensor("xo", [OWN, D], F32, kind="ExternalInput").ap()
    d_wq = nc.dram_tensor("wq", [D, D], F32, kind="ExternalInput").ap()
    d_bq = nc.dram_tensor("bq", [D], F32, kind="ExternalInput").ap()
    d_wqa = nc.dram_tensor("wqa", [D, H], F32, kind="ExternalInput").ap()
    d_bqa = nc.dram_tensor("bqa", [H], F32, kind="ExternalInput").ap()
    d_wk = nc.dram_tensor("wk", [D, D], F32, kind="ExternalInput").ap()
    d_bk = nc.dram_tensor("bk", [D], F32, kind="ExternalInput").ap()
    d_wka = nc.dram_tensor("wka", [D, H], F32, kind="ExternalInput").ap()
    d_bka = nc.dram_tensor("bka", [H], F32, kind="ExternalInput").ap()
    d_wt = nc.dram_tensor("wt", [D, D], F32, kind="ExternalInput").ap()
    d_bt = nc.dram_tensor("bt", [D], F32, kind="ExternalInput").ap()
    d_rww = nc.dram_tensor("rww", [D, 8], F32, kind="ExternalInput").ap()
    d_rwb = nc.dram_tensor("rwb", [8], F32, kind="ExternalInput").ap()
    d_sw1 = nc.dram_tensor("sw1", [D, HF], F32, kind="ExternalInput").ap()
    d_sb1 = nc.dram_tensor("sb1", [HF], F32, kind="ExternalInput").ap()
    d_sw2 = nc.dram_tensor("sw2", [HF, D], F32, kind="ExternalInput").ap()
    d_sb2 = nc.dram_tensor("sb2", [D], F32, kind="ExternalInput").ap()
    d_rw1 = nc.dram_tensor("rw1", [8, D, HF], F32, kind="ExternalInput").ap()
    d_rb1 = nc.dram_tensor("rb1", [8, HF], F32, kind="ExternalInput").ap()
    d_rw2 = nc.dram_tensor("rw2", [8, HF, D], F32, kind="ExternalInput").ap()
    d_rb2 = nc.dram_tensor("rb2", [8, D], F32, kind="ExternalInput").ap()

    d_out = nc.dram_tensor("out", [OWN, D], F32, kind="ExternalOutput").ap()

    d_xfm = nc.dram_tensor("s_xfm", [NCH, 128, SB], F32).ap()
    d_xtm = nc.dram_tensor("s_xtm", [SB, D], F32).ap()
    d_wqh = nc.dram_tensor("s_wqh", [NCH, 128, D], BF16).ap()
    d_wql = nc.dram_tensor("s_wql", [NCH, 128, D], BF16).ap()
    d_w1h = nc.dram_tensor("s_w1h", [NE, NCH, 128, HF], BF16).ap()
    d_w1l = nc.dram_tensor("s_w1l", [NE, NCH, 128, HF], BF16).ap()
    d_w2h = nc.dram_tensor("s_w2h", [NE, HF // 128, 128, D], BF16).ap()
    d_w2l = nc.dram_tensor("s_w2l", [NE, HF // 128, 128, D], BF16).ap()

    dbg = {}
    if cfg.debug:
        for nm, shp in [("dbg_pq", [128, 8]), ("dbg_pk", [128, 8]),
                        ("dbg_s1", [16, 1]), ("dbg_s2", [16, 1]),
                        ("dbg_e1", [16, D]), ("dbg_e2", [16, D]),
                        ("dbg_a", [128, 128]), ("dbg_c", [128, 128])]:
            dbg[nm] = nc.dram_tensor(nm, shp, F32, kind="ExternalOutput").ap()

    es = contextlib.ExitStack()
    with es:
        def sb(name, shape, dt=F32):
            return es.enter_context(nc.sbuf_tensor(name, shape, dt))

        ID = sb("ID", [128, 128])
        IDH = sb("IDH", [128, 128], BF16)
        PIV = sb("PIV", [128, 1])
        ONES1 = sb("ONES1", [1, 128])
        ONESH = sb("ONESH", [1, 128], BF16)
        WTH = sb("WTH", [128, NCH * D], BF16)
        WTL = sb("WTL", [128, NCH * D], BF16)
        RWH8 = sb("RWH8", [128, 64], BF16)
        RWL8 = sb("RWL8", [128, 64], BF16)
        Wqa8 = sb("Wqa8", [128, 128])
        Wka8 = sb("Wka8", [128, 128])
        WKA2 = sb("WKA2", [128, 128])
        A8 = sb("A8", [128, 128])
        C8 = sb("C8", [128, 128])
        PQ8 = sb("PQ8", [128, 8])
        PK8 = sb("PK8", [128, 8])
        BQ8 = sb("BQ8", [128, 8])
        BK8 = sb("BK8", [128, 8])
        BQAROW = sb("BQAROW", [1, H])
        BKAROW = sb("BKAROW", [1, H])
        BTROWH = sb("BTROWH", [1, D], BF16)
        BTROWL = sb("BTROWL", [1, D], BF16)
        RBROW = sb("RBROW", [1, 8])
        A0S = sb("A0S", [16, 1])
        C0S = sb("C0S", [16, 1])
        S1R = sb("S1R", [16, 1])
        S2R = sb("S2R", [16, 1])
        S1REC = sb("S1REC", [16, 1])
        S2REC = sb("S2REC", [16, 1])
        SROW = sb("SROW", [1, 16])
        TROW = sb("TROW", [1, 16])
        SML = sb("SML", [128, 32])
        CONSTS = sb("CONSTS", [128, 1])
        nc.const_aps.register(F32, 1e-5, CONSTS[:, 0:1])
        XB = [sb(f"XB{i}", [128, D]) for i in range(2)]
        XNT = [sb(f"XNT{i}", [128, D]) for i in range(2)]

        AR = Arena(nc, es, "ARENA", 149 * 1024)
        XNFM = [AR.at("s", f"XNFM{i}", [128, NCH * 512], F32) for i in range(2)]
        WSTG = [AR.at("s", f"WSTG{i}", [128, HF], F32) for i in range(2)]
        WHS = [AR.at("s", f"WHS{i}", [128, HF], BF16) for i in range(2)]
        WLS = [AR.at("s", f"WLS{i}", [128, HF], BF16) for i in range(2)]
        WSTR = [AR.at("s", f"WSTR{i}", [128, D], F32) for i in range(2)]
        WQT = AR.at("s", "WQT", [128, D], F32)
        E1SB = AR.at("s", "E1SB", [16, D], F32)
        E2SB = AR.at("s", "E2SB", [16, D], F32)
        POOL = AR.at("s", "POOL", [16, D], F32)
        S1COL = AR.at("s", "S1COL", [16, cfg.nT], F32)
        S2COL = AR.at("s", "S2COL", [16, cfg.nT], F32)
        E1T = AR.at("s", "E1T", [128, 128], F32)
        ETM = [AR.at("s", f"ETM{i}", [128, 16], F32) for i in range(2)]
        EHM = [AR.at("s", f"EHM{i}", [16, 128], F32) for i in range(2)]
        BQROW = AR.at("s", "BQROW", [1, D], F32)
        SXB = [AR.at("s", f"SXB{i}", [128, D], F32) for i in range(4)]
        SXNT = [AR.at("s", f"SXNT{i}", [128, D], F32) for i in range(4)]
        BKROW = AR.at("s", "BKROW", [1, D], F32)
        XNH = AR.at("q", "XNH", [128, NCH * 1024], BF16)
        XNL = AR.at("q", "XNL", [128, NCH * 1024], BF16)
        GACC = AR.at("q", "GACC", [128, 8 * D], F32)
        RWQA = AR.at("q", "RWQA", [128, 2 * NCH * 1024], BF16)
        RWQB = AR.at("q", "RWQB", [128, 2 * 4 * D], BF16)
        H1H = AR.at("q", "H1H", [128, 4 * 512], BF16)
        H1L = AR.at("q", "H1L", [128, 4 * 512], BF16)
        T0 = [AR.at("q", "T00", [128, 512], F32)]
        T0 = [T0[0], T0[0]]
        W8 = AR.at("q", "W8", [128, 8 * 8], F32)
        WT9 = AR.at("q", "WT9", [8, 1024], F32)
        MQHB = AR.at("q", "MQHB", [128, NCH * 128], BF16)
        MQLB = AR.at("q", "MQLB", [128, NCH * 128], BF16)
        WVH = AR.at("q", "WVH", [128, NCH * 128], BF16)
        WVL = AR.at("q", "WVL", [128, NCH * 128], BF16)
        TMQF = [AR.at("q", f"TMQF{i}", [128, 128], F32) for i in range(2)]
        RL = AR.at("q", "RL", [128, 8], F32)
        RP = AR.at("q", "RP", [128, 8], F32)
        RM = AR.at("q", "RM", [128, 2], F32)
        OUTB = AR.at("q", "OUTB", [128, D], F32)
        RB2SB2 = AR.at("q", "RB2SB2", [8, D], F32)
        SB2ROW = AR.at("q", "SB2ROW", [1, D], F32)
        BIAS1 = AR.at("q", "BIAS1", [128, 16 * NE], F32)

        PS = [es.enter_context(nc.psum_tensor(f"PS{i}", [128, 1024], F32))
              for i in range(4)]

        P = Prog(nc, es)
        for s in ("st", "pe", "dv", "sc", "gp"):
            P.sem(s)
        block = es.enter_context(nc.Block())

        def dma(eng, out, in_, reads=(), writes=(), extra=(), sem=None,
                slow=False, wguard=True):
            if sem is None:
                key = writes[0] if writes else (reads[0] if reads else "misc")
                sem = "D_" + key
            if slow:
                fn = lambda E, o=out, i=in_: E.dma_start(
                    out=o, in_=i, allow_slow_non_contiguous=True)
            else:
                fn = lambda E, o=out, i=in_: E.dma_start(out=o, in_=i)
            return P.op(eng, fn, reads=reads, writes=writes, extra=extra,
                        inc=sem, k=16, wguard=wguard)

        P.op("gpsimd", lambda E: E.iota(
            out=ID[:], pattern=[[1, 128]], base=0, channel_multiplier=0,
            allow_small_or_imprecise_dtypes=True), writes=["ID"], inc="gp")
        P.op("gpsimd", lambda E: E.iota(
            out=PIV[:], pattern=[[0, 1]], base=0, channel_multiplier=1,
            allow_small_or_imprecise_dtypes=True), writes=["PIV"], inc="gp")
        P.op("vector", lambda E: E.tensor_scalar(
            out=ID[:], in0=ID[:], scalar1=PIV[:], scalar2=None,
            op0=ALU.is_equal), reads=["ID", "PIV"], writes=["ID"], inc="dv")
        P.drain("vector")
        P.op("vector", lambda E: E.tensor_copy(out=IDH[:], in_=ID[:]),
             reads=["ID"], writes=["IDH"], inc="dv")
        P.op("vector", lambda E: E.memset(CONSTS[:, 0:1], 1e-5),
             writes=["CONSTS"])
        P.op("vector", lambda E: E.memset(ONES1[:], 1.0), writes=["ONES1"])
        P.op("vector", lambda E: E.memset(ONESH[:], 1.0), writes=["ONESH"],
             inc="dv")

        dma("sync", BQ8[:], d_bq.rearrange("(c p) -> p c", p=128),
            writes=["BQ8"], slow=True)
        dma("sync", BK8[:], d_bk.rearrange("(c p) -> p c", p=128),
            writes=["BK8"], slow=True)
        dma("sync", BQROW[:], d_bq.rearrange("(o d) -> o d", o=1), writes=["BQROW"])
        dma("sync", BKROW[:], d_bk.rearrange("(o d) -> o d", o=1), writes=["BKROW"])
        dma("sync", BQAROW[:], d_bqa.rearrange("(o d) -> o d", o=1), writes=["BQAROW"])
        dma("sync", BKAROW[:], d_bka.rearrange("(o d) -> o d", o=1), writes=["BKAROW"])
        dma("sync", RBROW[:], d_rwb.rearrange("(o d) -> o d", o=1), writes=["RBROW"])
        for c in range(NCH):
            dma("sync", Wqa8[:, 16 * c:16 * (c + 1)],
                d_wqa[128 * c:128 * (c + 1), :], writes=["Wqa8"])
            dma("sync", Wka8[:, 16 * c:16 * (c + 1)],
                d_wka[128 * c:128 * (c + 1), :], writes=["Wka8"])
        def barrier():
            mpe = P.drain("tensor", inc="pe")
            mdv = P.drain("vector", inc="dv")
            msc = P.drain("scalar", inc="sc")
            mgp = P.drain("gpsimd", inc="gp")
            waits = [("pe", mpe), ("dv", mdv), ("sc", msc), ("gp", mgp)]
            waits += [(nm, c) for nm, c in P.cnt.items()
                      if (nm.startswith("D_") or nm == "st") and c > 0]
            for eng in ENGS:
                P._raw(eng, lambda E: E.nop(), waits=waits, inc=None, k=1)

        for k in range(NCH):
            dma("sync", WSTR[k % 2][:], d_wq[128 * k:128 * (k + 1), :],
                writes=[f"WSTR{k % 2}"])
            for j in range(NCH):
                mk = P.op("tensor", lambda E, k=k, j=j: E.transpose(
                    out=PS[2][:, 128 * j:128 * (j + 1)],
                    in_=WSTR[k % 2][:, 128 * j:128 * (j + 1)], identity=ID[:]),
                    reads=[f"WSTR{k % 2}", "ID"],
                    writes=["PS2A", "PS2B"],
                    inc="pe" if j == NCH - 1 else None)
            P.op("vector", lambda E: E.tensor_copy(out=WQT[:], in_=PS[2][:]),
                 reads=["PS2A", "PS2B"], writes=["WQT"], inc="dv")
            for j in range(NCH):
                mk = P.op("tensor", lambda E, k=k, j=j: E.matmul(
                    out=PS[1][0:128, 0:16], lhsT=WQT[:, 128 * j:128 * (j + 1)],
                    rhs=Wqa8[:, 16 * j:16 * (j + 1)],
                    start=(j == 0), stop=(j == NCH - 1)),
                    reads=["WQT", "Wqa8"], writes=["PS1A"],
                    inc="pe" if j == NCH - 1 else None)
            P.op("vector", lambda E, k=k: E.tensor_copy(
                out=A8[:, 16 * k:16 * (k + 1)], in_=PS[1][0:128, 0:16]),
                reads=["PS1A"], writes=["A8"], inc="dv")
            b2 = k % 2
            P.op("scalar", lambda E, k=k, b2=b2: E.copy(
                out=WHS[b2][:, 0:D], in_=WSTR[b2][:]),
                reads=[f"WSTR{b2}"], writes=[f"WHS{b2}"], inc="sc")
            P.op("vector", lambda E, k=k, b2=b2: E.tensor_tensor(
                out=WLS[b2][:, 0:D], in0=WSTR[b2][:],
                in1=WHS[b2][:, 0:D], op=ALU.subtract),
                reads=[f"WSTR{b2}", f"WHS{b2}"], writes=[f"WLS{b2}"], inc="dv")
            dma("gpsimd", d_wqh[k], WHS[b2][:, 0:D], reads=[f"WHS{b2}"],
                writes=["d_wqh"], sem=f"D_spl{b2}")
            dma("gpsimd", d_wql[k], WLS[b2][:, 0:D],
                reads=[f"WLS{b2}", f"WHS{b2}"],
                writes=["d_wql", "d_wqh"], sem=f"D_spl{b2}")
        for j in range(NCH):
            P.op("tensor", lambda E, j=j: E.matmul(
                out=PS[1][0:1, 512:528], lhsT=BQ8[:, j:j + 1],
                rhs=Wqa8[:, 16 * j:16 * (j + 1)],
                start=(j == 0), stop=(j == NCH - 1)),
                reads=["BQ8", "Wqa8"], writes=["PS1B"],
                inc="pe" if j == NCH - 1 else None)
        P.op("vector", lambda E: E.tensor_tensor(
            out=TROW[:], in0=PS[1][0:1, 512:528], in1=BQAROW[:], op=ALU.add),
            reads=["PS1B", "BQAROW"], writes=["TROW"], inc="dv")
        P.op("tensor", lambda E: E.transpose(
            out=PS[1][0:16, 528:529], in_=TROW[:], identity=ID[0:1, 0:1]),
            reads=["TROW", "ID"], writes=["PS1B"], inc="pe")
        P.op("vector", lambda E: E.tensor_scalar(
            out=A0S[:], in0=PS[1][0:16, 528:529], scalar1=0.125, scalar2=None,
            op0=ALU.mult), reads=["PS1B"], writes=["A0S"], inc="dv")

        split_jobs = []
        for k in range(NCH):
            split_jobs.append(("res", d_wt[128 * k:128 * (k + 1), :], D,
                               WTH, WTL, D * k, "WTH", "WTL"))
        split_jobs.append(("resm", [d_rww[128 * c:128 * (c + 1), :]
                                    for c in range(NCH)],
                           64, RWH8, RWL8, 0, "RWH8", "RWL8"))
        for e in range(NE):
            s1 = d_sw1 if e == 8 else d_rw1[e]
            for c in range(NCH):
                split_jobs.append(("dram", s1[128 * c:128 * (c + 1), :], HF,
                                   d_w1h[e, c], d_w1l[e, c], None, "w1h", "w1l"))
            s2 = d_sw2 if e == 8 else d_rw2[e]
            for c in range(HF // 128):
                split_jobs.append(("dram", s2[128 * c:128 * (c + 1), :], D,
                                   d_w2h[e, c], d_w2l[e, c], None, "w2h", "w2l"))

        def emit_split(u):
            kind, src, width, dh, dl, off, kh, kl = split_jobs[u]
            b = u % 2
            if kind == "resm":
                w0 = width // len(src)
                for ci, sap in enumerate(src):
                    dma("sync", WSTG[b][:, w0 * ci:w0 * (ci + 1)], sap,
                        writes=[f"WSTG{b}"])
                kind = "res"
            else:
                dma("sync", WSTG[b][:, 0:width], src, writes=[f"WSTG{b}"])
            if kind == "res":
                P.op("scalar", lambda E: E.copy(
                    out=dh[:, off:off + width], in_=WSTG[b][:, 0:width]),
                    reads=[f"WSTG{b}"], writes=[kh], inc="sc")
                P.op("vector", lambda E: E.tensor_tensor(
                    out=dl[:, off:off + width], in0=WSTG[b][:, 0:width],
                    in1=dh[:, off:off + width], op=ALU.subtract),
                    reads=[f"WSTG{b}", kh], writes=[kl], inc="dv")
            else:
                P.op("scalar", lambda E: E.copy(
                    out=WHS[b][:, 0:width], in_=WSTG[b][:, 0:width]),
                    reads=[f"WSTG{b}"], writes=[f"WHS{b}"], inc="sc")
                P.op("vector", lambda E: E.tensor_tensor(
                    out=WLS[b][:, 0:width], in0=WSTG[b][:, 0:width],
                    in1=WHS[b][:, 0:width], op=ALU.subtract),
                    reads=[f"WSTG{b}", f"WHS{b}"], writes=[f"WLS{b}"], inc="dv")
                dma("gpsimd", dh, WHS[b][:, 0:width], reads=[f"WHS{b}"],
                    writes=[kh], sem=f"D_spl{b}")
                dma("gpsimd", dl, WLS[b][:, 0:width],
                    reads=[f"WLS{b}", f"WHS{b}"],
                    writes=[kl, kh], sem=f"D_spl{b}")

        def emit_ln(src_ap, xkey, xbuf, dkey, dst, par=0):
            SL = SML[:, 8 * par:8 * par + 8]
            skey = f"SML{par}"
            dma("sync", xbuf[:], src_ap, writes=[xkey])
            P.op("vector", lambda E: E.tensor_reduce(
                out=SL[:, 0:1], in_=xbuf[:], axis=AX.X, op=ALU.add),
                reads=[xkey], writes=[skey], inc="dv")
            P.op("scalar", lambda E: E.activation(
                out=dst[:], in_=xbuf[:], func=AF.Square, bias=0.0, scale=1.0,
                accum_out=SL[:, 4:5]),
                reads=[xkey], writes=[dkey, skey], inc="sc")
            P.op("vector", lambda E: E.tensor_scalar(
                out=SL[:, 1:2], in0=SL[:, 0:1], scalar1=1.0 / D,
                scalar2=None, op0=ALU.mult), reads=[skey], writes=[skey])
            P.drain("vector")
            P.op("vector", lambda E: E.tensor_tensor(
                out=SL[:, 2:3], in0=SL[:, 1:2], in1=SL[:, 1:2],
                op=ALU.mult))
            P.op("vector", lambda E: E.tensor_scalar(
                out=SL[:, 3:4], in0=SL[:, 4:5], scalar1=1.0 / D,
                scalar2=None, op0=ALU.mult))
            P.drain("vector")
            P.op("vector", lambda E: E.tensor_tensor(
                out=SL[:, 3:4], in0=SL[:, 3:4], in1=SL[:, 2:3],
                op=ALU.subtract), writes=[skey], inc="dv")
            P.op("scalar", lambda E: E.activation(
                out=SL[:, 5:6], in_=SL[:, 3:4], func=AF.Sqrt, bias=1e-5,
                scale=1.0), reads=[skey], writes=[skey], inc="sc")
            P.op("vector", lambda E: E.reciprocal(
                out=SL[:, 6:7], in_=SL[:, 5:6]),
                reads=[skey], writes=[skey])
            P.drain("vector")
            m = P.op("vector", lambda E: E.tensor_scalar(
                out=dst[:], in0=xbuf[:], scalar1=SL[:, 1:2],
                scalar2=SL[:, 6:7], op0=ALU.subtract, op1=ALU.mult),
                reads=[xkey, skey], writes=[dkey], inc="dv")
            return m

        def emit_spine(pass1):
            def emit_tail(t):
                tb4 = t % 4
                tb = t % 2
                qo = 512 * (t % 2)
                rkey = "PS1A" if t % 2 == 0 else "PS1B"
                P.op("tensor", lambda E, tb=tb, qo=qo: E.transpose(
                    out=PS[1][0:128, qo + 128:qo + 144], in_=EHM[tb][:],
                    identity=ID[0:16, 0:16]),
                    reads=[f"EHM{tb}", "ID"], writes=[rkey], inc="pe")
                P.op("vector", lambda E, tb=tb, qo=qo: E.tensor_copy(
                    out=ETM[tb][:], in_=PS[1][0:128, qo + 128:qo + 144]),
                    reads=[rkey], writes=[f"ETM{tb}"], inc="dv")
                for h in range(2):
                    P.op("tensor", lambda E, tb4=tb4, tb=tb, h=h, t=t: E.matmul(
                        out=PS[0][0:16, 512 * h:512 * (h + 1)], lhsT=ETM[tb][:],
                        rhs=SXNT[tb4][:, 512 * h:512 * (h + 1)],
                        start=(t == 0), stop=(t == cfg.nT - 1)),
                        reads=[f"ETM{tb}", f"SXNT{tb4}"],
                        writes=["PS0A" if h == 0 else "PS0B"],
                        inc="pe" if h == 1 else None)

            AB = A8 if pass1 else C8
            BSC = A0S if pass1 else C0S
            SCOL = S1COL if pass1 else S2COL
            for t in range(cfg.nT):
                g, tb4, tb, gb = t // 4, t % 4, t % 2, (t // 4) % 2
                c0 = 128 * (t % 4)
                qo = 512 * (t % 2)
                if pass1:
                    emit_ln(d_xb[128 * t:128 * (t + 1), :], f"SXB{tb4}",
                            SXB[tb4], f"SXNT{tb4}", SXNT[tb4], par=tb4)
                    dma("gpsimd", d_xtm[128 * t:128 * (t + 1), :],
                        SXNT[tb4][:], reads=[f"SXNT{tb4}"], writes=["xtm"],
                        sem=f"D_xtm{tb}")
                    pst = PS[2 + (t % 2)]
                    pkeys = [f"PS{2 + (t % 2)}A", f"PS{2 + (t % 2)}B"]
                    for j in range(NCH):
                        P.op("tensor", lambda E, j=j, pst=pst, tb4=tb4:
                             E.transpose(
                                 out=pst[:, 128 * j:128 * (j + 1)],
                                 in_=SXNT[tb4][:, 128 * j:128 * (j + 1)],
                                 identity=ID[:]),
                             reads=[f"SXNT{tb4}", "ID"], writes=pkeys,
                             inc="pe" if j == NCH - 1 else None)
                    for j in range(NCH):
                        P.op("scalar", lambda E, j=j, pst=pst, gb=gb, c0=c0:
                             E.copy(out=XNFM[gb][:, 512 * j + c0:
                                                 512 * j + c0 + 128],
                                    in_=pst[:, 128 * j:128 * (j + 1)]),
                             reads=pkeys, writes=[f"XNFM{gb}"],
                             inc="sc" if j == NCH - 1 else None)
                else:
                    if t % 4 == 0:
                        for j in range(NCH):
                            dma("gpsimd", XNFM[gb][:, 512 * j:512 * (j + 1)],
                                d_xfm[j, :, 512 * g:512 * (g + 1)],
                                reads=["xfm"], writes=[f"XNFM{gb}"],
                                wguard=(j == 0))
                    dma("sync", SXNT[tb4][:], d_xtm[128 * t:128 * (t + 1), :],
                        reads=["xtm"], writes=[f"SXNT{tb4}"])
                rkey = "PS1A" if t % 2 == 0 else "PS1B"
                for j in range(NCH):
                    P.op("tensor", lambda E, j=j, qo=qo, gb=gb, c0=c0, AB=AB:
                         E.matmul(
                             out=PS[1][0:16, qo:qo + 128],
                             lhsT=AB[:, 16 * j:16 * (j + 1)],
                             rhs=XNFM[gb][:, 512 * j + c0:512 * j + c0 + 128],
                             start=(j == 0), stop=(j == NCH - 1)),
                         reads=[f"XNFM{gb}", "A8" if pass1 else "C8"],
                         writes=[rkey],
                         inc="pe" if j == NCH - 1 else None)
                P.op("scalar", lambda E, t=t, tb=tb, qo=qo, BSC=BSC, SCOL=SCOL:
                     E.activation(out=EHM[tb][:], in_=PS[1][0:16, qo:qo + 128],
                                  func=AF.Exp, bias=BSC[:], scale=0.125,
                                  accum_out=SCOL[:, t:t + 1]),
                     reads=[rkey, "A0S" if pass1 else "C0S"],
                     writes=[f"EHM{tb}", "S1COL" if pass1 else "S2COL"],
                     inc="sc")
                if t >= 1:
                    emit_tail(t - 1)
                if pass1:
                    if t % 4 == 3:
                        for j in range(NCH):
                            dma("gpsimd", d_xfm[j, :, 512 * g:512 * (g + 1)],
                                XNFM[gb][:, 512 * j:512 * (j + 1)],
                                reads=[f"XNFM{gb}"], writes=["xfm"],
                                sem=f"D_xfm{gb}", wguard=(j == 0))
                    quota = ((t + 1) * len(split_jobs)) // cfg.nT
                    while emit_spine.spl_u < quota:
                        emit_split(emit_spine.spl_u)
                        emit_spine.spl_u += 1
            emit_tail(cfg.nT - 1)
            SR = S1R if pass1 else S2R
            SREC = S1REC if pass1 else S2REC
            ESB = E1SB if pass1 else E2SB
            P.op("vector", lambda E, SR=SR, SCOL=SCOL: E.tensor_reduce(
                out=SR[:], in_=SCOL[:], axis=AX.X, op=ALU.add),
                reads=["S1COL" if pass1 else "S2COL"],
                writes=["S1R" if pass1 else "S2R"], inc="dv")
            P.drain("vector")
            P.op("vector", lambda E, SR=SR, SREC=SREC: E.reciprocal(
                out=SREC[:], in_=SR[:]),
                reads=["S1R" if pass1 else "S2R"],
                writes=["S1REC" if pass1 else "S2REC"], inc="dv")
            P.op("vector", lambda E, ESB=ESB: E.tensor_copy(
                out=ESB[:], in_=PS[0][0:16, :]),
                reads=["PS0A", "PS0B"], writes=["E1SB" if pass1 else "E2SB"],
                inc="dv")

        emit_spine.spl_u = 0
        emit_spine(True)

        def emit_pool(tag, ESBk, ESB, SR, SRECk, SREC, brow_key, brow, PX_key,
                      PX, mul_pq):
            skey = "S1R" if tag == "q" else "S2R"
            P.op("tensor", lambda E: E.transpose(
                out=PS[1][0:1, 0:16], in_=SR[:], identity=ID[0:16, 0:16]),
                reads=[skey, "ID"], writes=["PS1A"], inc="pe")
            P.op("vector", lambda E: E.tensor_copy(
                out=SROW[:], in_=PS[1][0:1, 0:16]),
                reads=["PS1A"], writes=["SROW"], inc="dv")
            for k in range(NCH):
                P.op("tensor", lambda E, k=k: E.transpose(
                    out=PS[1][0:128, 16 * k + 32:16 * k + 48],
                    in_=ESB[:, 128 * k:128 * (k + 1)],
                    identity=ID[0:16, 0:16]),
                    reads=[ESBk, "ID"], writes=["PS1A"], inc="pe")
                P.op("vector", lambda E, k=k: E.tensor_copy(
                    out=E1T[:, 16 * k:16 * (k + 1)],
                    in_=PS[1][0:128, 16 * k + 32:16 * k + 48]),
                    reads=["PS1A"], writes=["E1T"], inc="dv")
            wsrc = d_wq if tag == "q" else d_wk
            for k in range(NCH):
                dma("sync", WSTR[k % 2][:], wsrc[128 * k:128 * (k + 1), :],
                    writes=[f"WSTR{k % 2}"])
                for h in range(2):
                    P.op("tensor", lambda E, k=k, h=h: E.matmul(
                        out=PS[0][0:16, 512 * h:512 * (h + 1)],
                        lhsT=E1T[:, 16 * k:16 * (k + 1)],
                        rhs=WSTR[k % 2][:, 512 * h:512 * (h + 1)],
                        start=(k == 0), stop=False),
                        reads=["E1T", f"WSTR{k % 2}"],
                        writes=["PS0A" if h == 0 else "PS0B"],
                        inc="pe" if h == 1 else None)
            for h in range(2):
                P.op("tensor", lambda E, h=h: E.matmul(
                    out=PS[0][0:16, 512 * h:512 * (h + 1)], lhsT=SROW[:],
                    rhs=brow[0:1, 512 * h:512 * (h + 1)],
                    start=False, stop=True),
                    reads=["SROW", brow_key],
                    writes=["PS0A" if h == 0 else "PS0B"],
                    inc="pe" if h == 1 else None)
            P.op("vector", lambda E: E.tensor_scalar(
                out=POOL[:], in0=PS[0][0:16, :], scalar1=SREC[:],
                scalar2=None, op0=ALU.mult),
                reads=["PS0A", "PS0B", SRECk], writes=["POOL"], inc="dv")
            for k in range(NCH):
                P.op("tensor", lambda E, k=k: E.transpose(
                    out=PS[1][0:128, 16 * k + 64:16 * k + 80],
                    in_=POOL[:, 128 * k:128 * (k + 1)],
                    identity=ID[0:16, 0:16]),
                    reads=["POOL", "ID"], writes=["PS1B"], inc="pe")
                P.op("vector", lambda E, k=k: E.tensor_copy(
                    out=PX[0:64, k:k + 1],
                    in_=PS[1][0:64, 16 * k + 64 + 2 * k:16 * k + 65 + 2 * k]),
                    reads=["PS1B"], writes=[PX_key])
                P.op("vector", lambda E, k=k: E.tensor_copy(
                    out=PX[64:128, k:k + 1],
                    in_=PS[1][64:128, 16 * k + 65 + 2 * k:16 * k + 66 + 2 * k]),
                    reads=["PS1B"], writes=[PX_key], inc="dv")
            if mul_pq:
                P.op("vector", lambda E: E.tensor_tensor(
                    out=PX[:], in0=PX[:], in1=PQ8[:], op=ALU.mult),
                    reads=[PX_key, "PQ8"], writes=[PX_key], inc="dv")

        emit_pool("q", "E1SB", E1SB, S1R, "S1REC", S1REC, "BQROW", BQROW,
                  "PQ8", PQ8, False)

        for k in range(NCH):
            P.op("vector", lambda E, k=k: E.tensor_scalar(
                out=WKA2[:, 16 * k:16 * (k + 1)],
                in0=Wka8[:, 16 * k:16 * (k + 1)],
                scalar1=PQ8[:, k:k + 1], scalar2=None, op0=ALU.mult),
                reads=["Wka8", "PQ8"], writes=["WKA2"],
                inc="dv" if k == NCH - 1 else None)
        for k in range(NCH):
            dma("sync", WSTR[k % 2][:], d_wk[128 * k:128 * (k + 1), :],
                writes=[f"WSTR{k % 2}"])
            for j in range(NCH):
                P.op("tensor", lambda E, k=k, j=j: E.transpose(
                    out=PS[2][:, 128 * j:128 * (j + 1)],
                    in_=WSTR[k % 2][:, 128 * j:128 * (j + 1)], identity=ID[:]),
                    reads=[f"WSTR{k % 2}", "ID"], writes=["PS2A", "PS2B"],
                    inc="pe" if j == NCH - 1 else None)
            P.op("vector", lambda E: E.tensor_copy(out=WQT[:], in_=PS[2][:]),
                 reads=["PS2A", "PS2B"], writes=["WQT"], inc="dv")
            for j in range(NCH):
                P.op("tensor", lambda E, k=k, j=j: E.matmul(
                    out=PS[1][0:128, 0:16], lhsT=WQT[:, 128 * j:128 * (j + 1)],
                    rhs=WKA2[:, 16 * j:16 * (j + 1)],
                    start=(j == 0), stop=(j == NCH - 1)),
                    reads=["WQT", "WKA2"], writes=["PS1A"],
                    inc="pe" if j == NCH - 1 else None)
            P.op("vector", lambda E, k=k: E.tensor_copy(
                out=C8[:, 16 * k:16 * (k + 1)], in_=PS[1][0:128, 0:16]),
                reads=["PS1A"], writes=["C8"], inc="dv")
        for j in range(NCH):
            P.op("tensor", lambda E, j=j: E.matmul(
                out=PS[1][0:1, 512:528], lhsT=BK8[:, j:j + 1],
                rhs=WKA2[:, 16 * j:16 * (j + 1)],
                start=(j == 0), stop=(j == NCH - 1)),
                reads=["BK8", "WKA2"], writes=["PS1B"],
                inc="pe" if j == NCH - 1 else None)
        P.op("vector", lambda E: E.tensor_tensor(
            out=TROW[:], in0=PS[1][0:1, 512:528], in1=BKAROW[:], op=ALU.add),
            reads=["PS1B", "BKAROW"], writes=["TROW"], inc="dv")
        P.op("tensor", lambda E: E.transpose(
            out=PS[1][0:16, 528:529], in_=TROW[:], identity=ID[0:1, 0:1]),
            reads=["TROW", "ID"], writes=["PS1B"], inc="pe")
        P.op("vector", lambda E: E.tensor_scalar(
            out=C0S[:], in0=PS[1][0:16, 528:529], scalar1=0.125, scalar2=None,
            op0=ALU.mult), reads=["PS1B"], writes=["C0S"], inc="dv")

        emit_spine(False)
        emit_pool("k", "E2SB", E2SB, S2R, "S2REC", S2REC, "BKROW", BKROW,
                  "PK8", PK8, True)

        if cfg.debug:
            dma("gpsimd", dbg["dbg_pq"], PQ8[:], reads=["PQ8"], sem="st")
            dma("gpsimd", dbg["dbg_pk"], PK8[:], reads=["PK8"], sem="st")
            dma("gpsimd", dbg["dbg_s1"], S1R[:], reads=["S1R"], sem="st")
            dma("gpsimd", dbg["dbg_s2"], S2R[:], reads=["S2R"], sem="st")
            dma("gpsimd", dbg["dbg_e1"], E1SB[:], reads=["E1SB"], sem="st")
            dma("gpsimd", dbg["dbg_e2"], E2SB[:], reads=["E2SB"], sem="st")
            dma("gpsimd", dbg["dbg_a"], A8[:], reads=["A8"], sem="st")
            dma("gpsimd", dbg["dbg_c"], C8[:], reads=["C8"], sem="st")

        barrier()
        dma("sync", RB2SB2[0:8, :], d_rb2, writes=["RB2SB2"])
        dma("sync", SB2ROW[:], d_sb2.rearrange("(o d) -> o d", o=1),
            writes=["SB2ROW"])
        for e in range(8):
            dma("sync", BIAS1[:, 16 * e:16 * e + 16],
                d_rb1[e].rearrange("(c p) -> p c", p=128),
                writes=["BIAS1"], slow=True)
        dma("sync", BIAS1[:, 16 * 8:16 * 9],
            d_sb1.rearrange("(c p) -> p c", p=128), writes=["BIAS1"], slow=True)
        dma("sync", OUTB[0:1, :], d_bt.rearrange("(o d) -> o d", o=1), writes=["OUTB"])
        P.op("vector", lambda E: E.tensor_copy(out=BTROWH[:], in_=OUTB[0:1, :]),
             reads=["OUTB"], writes=["BTROWH"], inc="dv")
        P.drain("vector")
        P.op("vector", lambda E: E.tensor_tensor(
            out=BTROWL[:], in0=OUTB[0:1, :], in1=BTROWH[:], op=ALU.subtract),
            reads=["OUTB", "BTROWH"], writes=["BTROWL"], inc="dv")

        HB = 4
        NHQ = HF // 512
        e_order = [8] + list(range(8))

        for q in range(cfg.NQ):
            q0 = q * cfg.Q

            for tb in range(cfg.tbs):
                t0c = 128 * tb
                b = tb % 2
                emit_ln(d_xo[q0 + t0c:q0 + t0c + 128, :], f"XB{b}", XB[b],
                        f"XNT{b}", XNT[b], par=b)
                pst = PS[2]
                pkeys = ["PS2A", "PS2B"]
                for j in range(NCH):
                    P.op("tensor", lambda E, j=j, pst=pst, b=b: E.transpose(
                        out=pst[:, 128 * j:128 * (j + 1)],
                        in_=XNT[b][:, 128 * j:128 * (j + 1)], identity=ID[:]),
                        reads=[f"XNT{b}", "ID"], writes=pkeys,
                        inc="pe" if j == NCH - 1 else None)
                for j in range(NCH):
                    P.op("scalar", lambda E, j=j, pst=pst, t0c=t0c: E.copy(
                        out=XNH[:, 1024 * j + t0c:1024 * j + t0c + 128],
                        in_=pst[:, 128 * j:128 * (j + 1)]),
                        reads=pkeys, writes=["XNH"],
                        inc="sc" if j == NCH - 1 else None)
                for j in range(NCH):
                    P.op("vector", lambda E, j=j, pst=pst, t0c=t0c:
                         E.tensor_tensor(
                             out=XNL[:, 1024 * j + t0c:1024 * j + t0c + 128],
                             in0=pst[:, 128 * j:128 * (j + 1)],
                             in1=XNH[:, 1024 * j + t0c:1024 * j + t0c + 128],
                             op=ALU.subtract),
                         reads=pkeys + ["XNH"], writes=["XNL"],
                         inc="dv" if j == NCH - 1 else None)
                P.op("tensor", lambda E: E.matmul(
                    out=PS[3][0:128, 0:8], lhsT=ONES1[:], rhs=RBROW[:],
                    start=True, stop=False),
                    reads=["ONES1", "RBROW"], writes=["PS3A"])
                for j in range(NCH):
                    for pi, (xa, wa, xk, wk_) in enumerate((
                            (XNH, RWH8, "XNH", "RWH8"),
                            (XNH, RWL8, "XNH", "RWL8"),
                            (XNL, RWH8, "XNL", "RWH8"))):
                        P.op("tensor", lambda E, j=j, xa=xa, wa=wa, t0c=t0c,
                             sp=(j == NCH - 1 and pi == 2):
                             E.matmul(
                                 out=PS[3][0:128, 0:8],
                                 lhsT=xa[:, 1024 * j + t0c:1024 * j + t0c + 128],
                                 rhs=wa[:, 8 * j:8 * (j + 1)],
                                 start=False, stop=sp),
                             reads=[xk, wk_], writes=["PS3A"],
                             inc="pe" if (j == NCH - 1 and pi == 2) else None)
                P.op("vector", lambda E: E.tensor_copy(
                    out=RL[:], in_=PS[3][0:128, 0:8]),
                    reads=["PS3A"], writes=["RL"], inc="dv")
                P.drain("vector")
                P.op("vector", lambda E: E.reduce_max(
                    out=RM[:, 0:1], in_=RL[:], axis=AX.X))
                P.drain("vector")
                P.op("vector", lambda E: E.tensor_scalar(
                    out=RP[:], in0=RL[:], scalar1=RM[:, 0:1], scalar2=None,
                    op0=ALU.is_equal))
                P.drain("vector")
                P.op("vector", lambda E: E.scalar_tensor_tensor(
                    out=RP[:], in0=RP[:], scalar=-1e30, in1=RL[:],
                    op0=ALU.mult, op1=ALU.add))
                P.drain("vector")
                P.op("vector", lambda E: E.reduce_max(
                    out=RM[:, 1:2], in_=RP[:], axis=AX.X, negate=True))
                P.drain("vector")
                P.op("vector", lambda E: E.tensor_scalar(
                    out=RP[:], in0=RL[:], scalar1=RM[:, 1:2], scalar2=0.0,
                    op0=ALU.add, op1=ALU.is_ge))
                P.drain("vector")
                P.op("vector", lambda E: E.reduce_max(
                    out=RM[:, 0:1], in_=RL[:], axis=AX.X, negate=True),
                    writes=["RM"], reads=["RL"], inc="dv")
                P.op("scalar", lambda E: E.activation(
                    out=RL[:], in_=RL[:], func=AF.Exp, bias=RM[:, 0:1],
                    scale=1.0, accum_out=RM[:, 1:2]),
                    reads=["RL", "RM"], writes=["RL", "RM"], inc="sc")
                P.op("vector", lambda E: E.tensor_tensor(
                    out=RP[:], in0=RP[:], in1=RL[:], op=ALU.mult),
                    reads=["RP", "RL"], writes=["RP"])
                P.drain("vector")
                P.op("vector", lambda E: E.reciprocal(
                    out=RM[:, 0:1], in_=RM[:, 1:2]), reads=["RM"],
                    writes=["RM"])
                P.drain("vector")
                P.op("vector", lambda E, tb=tb: E.tensor_scalar(
                    out=W8[:, 8 * tb:8 * (tb + 1)], in0=RP[:],
                    scalar1=RM[:, 0:1], scalar2=None, op0=ALU.mult),
                    reads=["RP", "RM"], writes=["W8"], inc="dv")
                P.op("tensor", lambda E, tb=tb: E.transpose(
                    out=PS[3][0:8, 512:640], in_=W8[:, 8 * tb:8 * (tb + 1)],
                    identity=ID[:]),
                    reads=["W8", "ID"], writes=["PS3B"], inc="pe")
                P.op("vector", lambda E, tb=tb: E.tensor_copy(
                    out=WT9[:, 128 * tb:128 * (tb + 1)],
                    in_=PS[3][0:8, 512:640]),
                    reads=["PS3B"], writes=["WT9"], inc="dv")

            for ei, e in enumerate(e_order):
                for hq in range(NHQ):
                    if hq % 2 == 0:
                        for c in range(NCH):
                            dma("sync", RWQA[:, 1024 * c:1024 * (c + 1)],
                                d_w1h[e, c, :, 1024 * (hq // 2):
                                      1024 * (hq // 2) + 1024],
                                reads=["w1h"], writes=["RWQA"],
                                wguard=(c == 0))
                            dma("sync",
                                RWQA[:, 8192 + 1024 * c:8192 + 1024 * (c + 1)],
                                d_w1l[e, c, :, 1024 * (hq // 2):
                                      1024 * (hq // 2) + 1024],
                                reads=["w1l"], writes=["RWQA"],
                                wguard=False)
                    for c in range(4):
                        dma("sync", RWQB[:, 1024 * c:1024 * (c + 1)],
                            d_w2h[e, 4 * hq + c], reads=["w2h"],
                            writes=["RWQB"], wguard=(c == 0))
                        dma("sync", RWQB[:, 4096 + 1024 * c:4096 + 1024 * (c + 1)],
                            d_w2l[e, 4 * hq + c], reads=["w2l"],
                            writes=["RWQB"], wguard=False)
                    for sg in range(cfg.sgs):
                        for hb in range(HB):
                            hbp = hb % 2
                            reg = PS[0][:, 512 * hbp:512 * hbp + 512]
                            rkey = "PS0A" if hbp == 0 else "PS0B"
                            for j in range(NCH):
                                xsl = slice(1024 * j + 512 * sg,
                                            1024 * j + 512 * sg + 512)
                                wsl = slice(1024 * j + 512 * (hq % 2) + 128 * hb,
                                            1024 * j + 512 * (hq % 2) + 128 * hb
                                            + 128)
                                for pi, (woff, xa, xk) in enumerate((
                                        (0, XNH, "XNH"),
                                        (0, XNL, "XNL"),
                                        (8192, XNH, "XNH"))):
                                    P.op("tensor",
                                         lambda E, woff=woff, xa=xa, wsl=wsl,
                                         xsl=xsl, reg=reg,
                                         st=(j == 0 and pi == 0),
                                         sp=(j == NCH - 1 and pi == 2):
                                         E.matmul(
                                             out=reg,
                                             lhsT=RWQA[:, woff + wsl.start:
                                                       woff + wsl.stop],
                                             rhs=xa[:, xsl],
                                             start=st, stop=sp),
                                         reads=["RWQA", xk], writes=[rkey],
                                         inc="pe" if (j == NCH - 1 and pi == 2)
                                         else None)
                            bcol = 16 * e + 4 * hq + hb
                            P.op("scalar", lambda E, reg=reg, hbp=hbp,
                                 bcol=bcol: E.activation(
                                     out=T0[hbp][:], in_=reg, func=AF.Gelu,
                                     bias=BIAS1[:, bcol:bcol + 1], scale=1.0),
                                 reads=[rkey, "BIAS1"], writes=["T00"],
                                 inc="sc")
                            hsl = slice(512 * hb, 512 * hb + 512)
                            P.op("vector", lambda E, hbp=hbp, hsl=hsl:
                                 E.tensor_copy(out=H1H[:, hsl],
                                               in_=T0[hbp][:]),
                                 reads=["T00"], writes=["H1H"],
                                 inc="dv")
                            P.drain("vector")
                            P.op("vector", lambda E, hbp=hbp, hsl=hsl:
                                 E.tensor_tensor(out=H1L[:, hsl],
                                                 in0=T0[hbp][:],
                                                 in1=H1H[:, hsl],
                                                 op=ALU.subtract),
                                 reads=["T00", "H1H"],
                                 writes=["H1L"], inc="dv")
                        first = (ei == 0 and hq == 0)
                        for tb4 in range(4):
                            gtb = 4 * sg + tb4
                            pst = PS[1 + (gtb % 2)]
                            pk = [f"PS{1 + (gtb % 2)}A", f"PS{1 + (gtb % 2)}B"]
                            if first:
                                for h in range(2):
                                    P.op("tensor", lambda E, h=h, gtb=gtb,
                                         pst=pst: E.matmul(
                                             out=pst[:, 512 * h:512 * (h + 1)],
                                             lhsT=WT9[:, 128 * gtb:
                                                      128 * (gtb + 1)],
                                             rhs=RB2SB2[:, 512 * h:
                                                        512 * (h + 1)],
                                             start=True, stop=False),
                                         reads=["WT9", "RB2SB2"], writes=pk)
                                    P.op("tensor", lambda E, h=h, pst=pst:
                                         E.matmul(
                                             out=pst[:, 512 * h:512 * (h + 1)],
                                             lhsT=ONES1[:, 0:128],
                                             rhs=SB2ROW[0:1, 512 * h:
                                                        512 * (h + 1)],
                                             start=False, stop=False),
                                         reads=["ONES1", "SB2ROW"], writes=pk)
                            for hc in range(HB):
                                hsl = slice(512 * hc + 128 * tb4,
                                            512 * hc + 128 * tb4 + 128)
                                for pi, (ha, woff, hk) in enumerate((
                                        (H1H, 0, "H1H"),
                                        (H1H, 4096, "H1H"),
                                        (H1L, 0, "H1L"))):
                                    for h in range(2):
                                        P.op("tensor",
                                             lambda E, ha=ha, woff=woff,
                                             hsl=hsl, h=h, hc=hc, pst=pst,
                                             st=(not first and hc == 0
                                                 and pi == 0),
                                             sp=(hc == HB - 1 and pi == 2):
                                             E.matmul(
                                                 out=pst[:, 512 * h:
                                                         512 * (h + 1)],
                                                 lhsT=ha[:, hsl],
                                                 rhs=RWQB[:, woff + 1024 * hc
                                                          + 512 * h:
                                                          woff + 1024 * hc
                                                          + 512 * h + 512],
                                                 start=st, stop=sp),
                                             reads=[hk, "RWQB"], writes=pk,
                                             inc="pe" if (hc == HB - 1 and
                                                          pi == 2 and h == 1)
                                             else None)
                            gk = f"G{gtb}"
                            if first:
                                P.op("vector", lambda E, pst=pst, gtb=gtb:
                                     E.tensor_copy(
                                         out=GACC[:, D * gtb:D * (gtb + 1)],
                                         in_=pst[:]),
                                     reads=pk, writes=[gk], inc="dv")
                            elif e == 8:
                                P.op("vector", lambda E, pst=pst, gtb=gtb:
                                     E.tensor_tensor(
                                         out=GACC[:, D * gtb:D * (gtb + 1)],
                                         in0=GACC[:, D * gtb:D * (gtb + 1)],
                                         in1=pst[:], op=ALU.add),
                                     reads=pk + [gk], writes=[gk], inc="dv")
                            else:
                                wcol = 8 * gtb + e
                                P.op("vector", lambda E, pst=pst, gtb=gtb,
                                     wcol=wcol: E.scalar_tensor_tensor(
                                         out=GACC[:, D * gtb:D * (gtb + 1)],
                                         in0=pst[:],
                                         scalar=W8[:, wcol:wcol + 1],
                                         in1=GACC[:, D * gtb:D * (gtb + 1)],
                                         op0=ALU.mult, op1=ALU.add),
                                     reads=pk + [gk, "W8"], writes=[gk],
                                     inc="dv")

            for gtb in range(cfg.tbs):
                P.op("scalar", lambda E, gtb=gtb: E.activation(
                    out=GACC[:, D * gtb:D * (gtb + 1)],
                    in_=GACC[:, D * gtb:D * (gtb + 1)],
                    func=AF.Silu, bias=0.0, scale=1.0),
                    reads=[f"G{gtb}"], writes=[f"G{gtb}"], inc="sc")

            for k in range(NCH):
                dma("sync", RWQA[:, 1024 * k:1024 * (k + 1)], d_wqh[k],
                    reads=["d_wqh"], writes=["RWQA"], wguard=(k == 0))
                dma("sync", RWQB[:, 1024 * k:1024 * (k + 1)], d_wql[k],
                    reads=["d_wql"], writes=["RWQB"], wguard=(k == 0))

            for gtb in range(cfg.tbs):
                t0c = 128 * gtb
                for fo in range(NCH):
                    reg = PS[0][:, 512 * (fo // 4) + 128 * (fo % 4):
                                512 * (fo // 4) + 128 * (fo % 4) + 128]
                    rkey = "PS0A" if fo < 4 else "PS0B"
                    for j in range(NCH):
                        xsl = slice(1024 * j + t0c, 1024 * j + t0c + 128)
                        wsl = slice(D * j + 128 * fo, D * j + 128 * fo + 128)
                        for pi, (wa, xa, wk_, xk) in enumerate((
                                (RWQA, XNH, "RWQA", "XNH"),
                                (RWQA, XNL, "RWQA", "XNL"),
                                (RWQB, XNH, "RWQB", "XNH"))):
                            P.op("tensor", lambda E, wa=wa, xa=xa, wsl=wsl,
                                 xsl=xsl, reg=reg,
                                 st=(j == 0 and pi == 0),
                                 sp=(j == NCH - 1 and pi == 2): E.matmul(
                                     out=reg, lhsT=wa[:, wsl], rhs=xa[:, xsl],
                                     start=st, stop=sp),
                                 reads=[wk_, xk], writes=[rkey],
                                 inc="pe" if (j == NCH - 1 and pi == 2)
                                 else None)
                    b = gtb % 2
                    P.op("scalar", lambda E, reg=reg, fo=fo, b=b:
                         E.activation(out=TMQF[b][:], in_=reg,
                                      func=AF.Identity,
                                      bias=BQ8[:, fo:fo + 1], scale=1.0),
                         reads=[rkey, "BQ8"], writes=[f"TMQF{b}"], inc="sc")
                    P.op("vector", lambda E, fo=fo, b=b: E.tensor_copy(
                        out=MQHB[:, 128 * fo:128 * (fo + 1)], in_=TMQF[b][:]),
                        reads=[f"TMQF{b}"], writes=["MQHB"], inc="dv")
                    P.drain("vector")
                    P.op("vector", lambda E, fo=fo, b=b: E.tensor_tensor(
                        out=MQLB[:, 128 * fo:128 * (fo + 1)], in0=TMQF[b][:],
                        in1=MQHB[:, 128 * fo:128 * (fo + 1)], op=ALU.subtract),
                        reads=[f"TMQF{b}", "MQHB"], writes=["MQLB"])
                    P.op("vector", lambda E, fo=fo, b=b: E.tensor_scalar(
                        out=WVH[:, 128 * fo:128 * (fo + 1)], in0=TMQF[b][:],
                        scalar1=PK8[:, fo:fo + 1], scalar2=None, op0=ALU.mult),
                        reads=[f"TMQF{b}", "PK8"], writes=["WVH"], inc="dv")
                    P.drain("vector")
                    P.op("vector", lambda E, fo=fo, b=b:
                         E.scalar_tensor_tensor(
                             out=WVL[:, 128 * fo:128 * (fo + 1)],
                             in0=TMQF[b][:], scalar=PK8[:, fo:fo + 1],
                             in1=WVH[:, 128 * fo:128 * (fo + 1)],
                             op0=ALU.mult, op1=ALU.subtract),
                         reads=[f"TMQF{b}", "PK8", "WVH"], writes=["WVL"],
                         inc="dv")
                pst = PS[1 + (gtb % 2)]
                pk = [f"PS{1 + (gtb % 2)}A", f"PS{1 + (gtb % 2)}B"]
                for pi, (bt_, bk_) in enumerate(((BTROWH, "BTROWH"),
                                                 (BTROWL, "BTROWL"))):
                    for h in range(2):
                        P.op("tensor", lambda E, bt_=bt_, h=h, pst=pst,
                             st=(pi == 0): E.matmul(
                                 out=pst[:, 512 * h:512 * (h + 1)],
                                 lhsT=ONESH[:],
                                 rhs=bt_[0:1, 512 * h:512 * (h + 1)],
                                 start=st, stop=False),
                             reads=["ONESH", bk_], writes=pk)
                for j in range(NCH):
                    vsl = slice(128 * j, 128 * (j + 1))
                    for pi, (va, wa, vk, wk_) in enumerate((
                            (WVH, WTH, "WVH", "WTH"),
                            (WVH, WTL, "WVH", "WTL"),
                            (WVL, WTH, "WVL", "WTH"))):
                        for h in range(2):
                            P.op("tensor", lambda E, va=va, wa=wa, vsl=vsl,
                                 j=j, h=h, pst=pst: E.matmul(
                                     out=pst[:, 512 * h:512 * (h + 1)],
                                     lhsT=va[:, vsl],
                                     rhs=wa[:, D * j + 512 * h:
                                            D * j + 512 * h + 512],
                                     start=False, stop=False),
                                 reads=[vk, wk_], writes=pk)
                for pi, (ma_, mkk) in enumerate(((MQHB, "MQHB"),
                                                 (MQLB, "MQLB"))):
                    for j in range(NCH):
                        is_last = (pi == 1 and (j == 3 or j == NCH - 1))
                        P.op("tensor", lambda E, ma_=ma_, j=j, pst=pst,
                             sp=is_last: E.matmul(
                                 out=pst[:, 128 * j:128 * (j + 1)],
                                 lhsT=ma_[:, 128 * j:128 * (j + 1)],
                                 rhs=IDH[:], start=False, stop=sp),
                             reads=[mkk, "IDH"], writes=pk,
                             inc="pe" if (pi == 1 and j == NCH - 1) else None)
                b = gtb % 2
                dma("sync", XB[b][:], d_xo[q0 + t0c:q0 + t0c + 128, :],
                    writes=[f"XB{b}"])
                P.op("vector", lambda E, pst=pst, gtb=gtb: E.tensor_tensor(
                    out=OUTB[:], in0=pst[:], in1=GACC[:, D * gtb:D * (gtb + 1)],
                    op=ALU.mult),
                    reads=pk + [f"G{gtb}"], writes=["OUTB"], inc="dv")
                P.drain("vector")
                P.op("vector", lambda E, b=b: E.tensor_tensor(
                    out=OUTB[:], in0=OUTB[:], in1=XB[b][:], op=ALU.add),
                    reads=["OUTB", f"XB{b}"], writes=["OUTB"], inc="dv")
                dma("gpsimd", d_out[q0 + t0c:q0 + t0c + 128, :], OUTB[:],
                    reads=["OUTB"], sem="st")

        for eng in ("vector", "scalar", "tensor"):
            P.drain(eng, inc={"vector": "dv", "scalar": "sc",
                              "tensor": None}[eng])
        P._raw("gpsimd", lambda E: E.nop(),
               waits=[("st", P.cnt["st"])], inc=None, k=1)
        if getattr(cfg, "verbose", False):
            print("arena views:", AR.cur, "arena size", AR.nbytes,
                  "sbuf remaining", nc.sbuf_bytes_remaining)
        P.replay(block)
    return nc


def kernel(**inputs):
    cfg = Cfg()
    x = np.ascontiguousarray(np.asarray(inputs["L0_attn_hidden_states"],
                                        dtype=np.float32))
    B, S, _ = x.shape
    w = {k: np.ascontiguousarray(np.asarray(v, dtype=np.float32))
         for k, v in inputs.items() if k != "L0_attn_hidden_states"}
    nc = build_nc(cfg)
    wmap = {
        "wq": w["L0_attn_Wq"], "bq": w["L0_attn_bq"],
        "wqa": w["L0_attn_Wqa"], "bqa": w["L0_attn_bqa"],
        "wk": w["L0_attn_Wk"], "bk": w["L0_attn_bk"],
        "wka": w["L0_attn_Wka"], "bka": w["L0_attn_bka"],
        "wt": w["L0_attn_Wt"], "bt": w["L0_attn_bt"],
        "rww": w["L0_ffn_router_w"], "rwb": w["L0_ffn_router_b"],
        "sw1": w["L0_ffn_sw1"], "sb1": w["L0_ffn_sb1"],
        "sw2": w["L0_ffn_sw2"], "sb2": w["L0_ffn_sb2"],
        "rw1": w["L0_ffn_rw1"], "rb1": w["L0_ffn_rb1"],
        "rw2": w["L0_ffn_rw2"], "rb2": w["L0_ffn_rb2"],
    }
    in_maps = []
    for c in range(8):
        b = c // 4
        r0 = (c % 4) * cfg.OWN
        im = {"xb": x[b], "xo": np.ascontiguousarray(x[b, r0:r0 + cfg.OWN])}
        im.update(wmap)
        in_maps.append(im)
    res = run_bass_kernel_spmd(nc, in_maps, core_ids=list(range(8)))
    out = np.empty((B, S, D), dtype=np.float32)
    for c in range(8):
        b = c // 4
        r0 = (c % 4) * cfg.OWN
        out[b, r0:r0 + cfg.OWN] = res.results[c]["out"]
    return out


# version 05
# speedup vs baseline: 1.0059x; 1.0059x over previous
import contextlib
import numpy as np

import concourse.bass as bass
import concourse.mybir as mybir
from concourse.bass_utils import run_bass_kernel_spmd

F32 = mybir.dt.float32
BF16 = mybir.dt.bfloat16
AF = mybir.ActivationFunctionType
ALU = mybir.AluOpType
AX = mybir.AxisListType

ENGS = ("sync", "tensor", "vector", "scalar", "gpsimd")

D = 1024
H = 16
HF = 2048
NE = 9
NCH = D // 128


class Cfg:
    def __init__(self, SB=16384, OWN=4096, debug=False):
        self.SB = SB
        self.OWN = OWN
        self.debug = debug
        self.nT = SB // 128
        self.NQ = max(1, OWN // 1024)
        self.Q = OWN // self.NQ
        self.tbs = self.Q // 128
        self.sgs = max(1, self.Q // 512)


class Arena:

    def __init__(self, nc, es, name, nbytes):
        self.nc = nc
        self.h = es.enter_context(
            nc.sbuf_tensor(name, [128, nbytes], mybir.dt.uint8))
        self.base = nc.lookup_mloc(self.h).addr
        self.nbytes = nbytes
        self.cur = {}

    def at(self, tag, name, shape, dt):
        per = 1
        for s in shape[1:]:
            per *= s
        per *= mybir.dt.size(dt)
        off = (self.cur.get(tag, 0) + 63) & ~63
        self.cur[tag] = off + per
        assert self.cur[tag] <= self.nbytes, (name, tag, self.cur[tag])
        return self.nc.alloc_sbuf_tensor_at(name, shape, dt,
                                            offset=self.base + off)


class Prog:
    def __init__(self, nc, es):
        self.nc = nc
        self.es = es
        self.q = {e: [] for e in ENGS}
        self.sems = {}
        self.cnt = {}
        self.last_wait = {e: {} for e in ENGS}
        self.use = {}
        self.wrt = {}
        self.pend = {e: [] for e in ENGS}

    def sem(self, name):
        if name not in self.sems:
            self.sems[name] = self.es.enter_context(self.nc.semaphore(name))
            self.cnt[name] = 0
        return name

    def _raw(self, eng, fn, waits, inc, k):
        ws = []
        for s, v in waits:
            if v is None:
                continue
            assert v <= self.cnt.get(s, 0), (s, v, self.cnt.get(s, 0))
            if self.last_wait[eng].get(s, -1) < v:
                ws.append((s, v))
                self.last_wait[eng][s] = v
        mark = None
        if inc is not None:
            self.sem(inc)
            self.cnt[inc] += k
            mark = self.cnt[inc]
        self.q[eng].append((fn, tuple(ws), inc, k))
        return mark

    def op(self, eng, fn, reads=(), writes=(), extra=(), inc=None, k=1,
           wguard=True):
        waits = []
        if wguard:
            for b in writes:
                waits += list(self.use.get(b, {}).items())
                waits += list(self.wrt.get(b, {}).items())
        for b in reads:
            waits += list(self.wrt.get(b, {}).items())
        for x in extra:
            if x is not None:
                waits.append(x)
        mark = self._raw(eng, fn, waits, inc, k)
        for b in writes:
            self.pend[eng].append(("w", b))
        for b in reads:
            self.pend[eng].append(("r", b))
        if inc is not None:
            for kind, b in self.pend[eng]:
                if kind == "w":
                    self.use[b] = {}
                    self.wrt[b] = {inc: mark}
                else:
                    u = self.use.setdefault(b, {})
                    u[inc] = mark
            self.pend[eng] = []
        return mark

    def drain(self, eng, inc=None):
        return self.op(eng, lambda E: E.drain(), inc=inc)

    def replay(self, block):
        for eng in ENGS:
            deco = getattr(block, eng)

            def make(eng):
                steps = self.q[eng]
                sems = self.sems

                def run(E, steps=steps, sems=sems):
                    for fn, ws, inc, k in steps:
                        for s, v in ws:
                            E.wait_ge(sems[s], v)
                        inst = fn(E)
                        if inc is not None:
                            inst.then_inc(sems[inc], k)
                return run
            deco(make(eng))


def build_nc(cfg: Cfg):
    nc = bass.Bass("TRN2", target_bir_lowering=False)
    SB, OWN = cfg.SB, cfg.OWN

    d_xb = nc.dram_tensor("xb", [SB, D], F32, kind="ExternalInput").ap()
    d_xo = nc.dram_tensor("xo", [OWN, D], F32, kind="ExternalInput").ap()
    d_wq = nc.dram_tensor("wq", [D, D], F32, kind="ExternalInput").ap()
    d_bq = nc.dram_tensor("bq", [D], F32, kind="ExternalInput").ap()
    d_wqa = nc.dram_tensor("wqa", [D, H], F32, kind="ExternalInput").ap()
    d_bqa = nc.dram_tensor("bqa", [H], F32, kind="ExternalInput").ap()
    d_wk = nc.dram_tensor("wk", [D, D], F32, kind="ExternalInput").ap()
    d_bk = nc.dram_tensor("bk", [D], F32, kind="ExternalInput").ap()
    d_wka = nc.dram_tensor("wka", [D, H], F32, kind="ExternalInput").ap()
    d_bka = nc.dram_tensor("bka", [H], F32, kind="ExternalInput").ap()
    d_wt = nc.dram_tensor("wt", [D, D], F32, kind="ExternalInput").ap()
    d_bt = nc.dram_tensor("bt", [D], F32, kind="ExternalInput").ap()
    d_rww = nc.dram_tensor("rww", [D, 8], F32, kind="ExternalInput").ap()
    d_rwb = nc.dram_tensor("rwb", [8], F32, kind="ExternalInput").ap()
    d_sw1 = nc.dram_tensor("sw1", [D, HF], F32, kind="ExternalInput").ap()
    d_sb1 = nc.dram_tensor("sb1", [HF], F32, kind="ExternalInput").ap()
    d_sw2 = nc.dram_tensor("sw2", [HF, D], F32, kind="ExternalInput").ap()
    d_sb2 = nc.dram_tensor("sb2", [D], F32, kind="ExternalInput").ap()
    d_rw1 = nc.dram_tensor("rw1", [8, D, HF], F32, kind="ExternalInput").ap()
    d_rb1 = nc.dram_tensor("rb1", [8, HF], F32, kind="ExternalInput").ap()
    d_rw2 = nc.dram_tensor("rw2", [8, HF, D], F32, kind="ExternalInput").ap()
    d_rb2 = nc.dram_tensor("rb2", [8, D], F32, kind="ExternalInput").ap()

    d_out = nc.dram_tensor("out", [OWN, D], F32, kind="ExternalOutput").ap()

    d_xfm = nc.dram_tensor("s_xfm", [NCH, 128, SB], F32).ap()
    d_xtm = nc.dram_tensor("s_xtm", [SB, D], F32).ap()
    d_wqh = nc.dram_tensor("s_wqh", [NCH, 128, D], BF16).ap()
    d_wql = nc.dram_tensor("s_wql", [NCH, 128, D], BF16).ap()
    d_w1h = nc.dram_tensor("s_w1h", [NE, NCH, 128, HF], BF16).ap()
    d_w1l = nc.dram_tensor("s_w1l", [NE, NCH, 128, HF], BF16).ap()
    d_w2h = nc.dram_tensor("s_w2h", [NE, HF // 128, 128, D], BF16).ap()
    d_w2l = nc.dram_tensor("s_w2l", [NE, HF // 128, 128, D], BF16).ap()

    dbg = {}
    if cfg.debug:
        for nm, shp in [("dbg_pq", [128, 8]), ("dbg_pk", [128, 8]),
                        ("dbg_s1", [16, 1]), ("dbg_s2", [16, 1]),
                        ("dbg_e1", [16, D]), ("dbg_e2", [16, D]),
                        ("dbg_a", [128, 128]), ("dbg_c", [128, 128])]:
            dbg[nm] = nc.dram_tensor(nm, shp, F32, kind="ExternalOutput").ap()

    es = contextlib.ExitStack()
    with es:
        def sb(name, shape, dt=F32):
            return es.enter_context(nc.sbuf_tensor(name, shape, dt))

        ID = sb("ID", [128, 128])
        IDH = sb("IDH", [128, 128], BF16)
        PIV = sb("PIV", [128, 1])
        ONES1 = sb("ONES1", [1, 128])
        ONESH = sb("ONESH", [1, 128], BF16)
        WTH = sb("WTH", [128, NCH * D], BF16)
        WTL = sb("WTL", [128, NCH * D], BF16)
        RWH8 = sb("RWH8", [128, 64], BF16)
        RWL8 = sb("RWL8", [128, 64], BF16)
        Wqa8 = sb("Wqa8", [128, 128])
        Wka8 = sb("Wka8", [128, 128])
        WKA2 = sb("WKA2", [128, 128])
        A8 = sb("A8", [128, 128])
        C8 = sb("C8", [128, 128])
        PQ8 = sb("PQ8", [128, 8])
        PK8 = sb("PK8", [128, 8])
        BQ8 = sb("BQ8", [128, 8])
        BK8 = sb("BK8", [128, 8])
        BQAROW = sb("BQAROW", [1, H])
        BKAROW = sb("BKAROW", [1, H])
        BTROWH = sb("BTROWH", [1, D], BF16)
        BTROWL = sb("BTROWL", [1, D], BF16)
        RBROW = sb("RBROW", [1, 8])
        A0S = sb("A0S", [16, 1])
        C0S = sb("C0S", [16, 1])
        S1R = sb("S1R", [16, 1])
        S2R = sb("S2R", [16, 1])
        S1REC = sb("S1REC", [16, 1])
        S2REC = sb("S2REC", [16, 1])
        SROW = sb("SROW", [1, 16])
        TROW = sb("TROW", [1, 16])
        SML = sb("SML", [128, 32])
        CONSTS = sb("CONSTS", [128, 1])
        nc.const_aps.register(F32, 1e-5, CONSTS[:, 0:1])
        XB = [sb(f"XB{i}", [128, D]) for i in range(2)]
        XNT = [sb(f"XNT{i}", [128, D]) for i in range(2)]

        AR = Arena(nc, es, "ARENA", 149 * 1024)
        XNFM = [AR.at("s", f"XNFM{i}", [128, NCH * 512], F32) for i in range(2)]
        WSTG = [AR.at("s", f"WSTG{i}", [128, HF], F32) for i in range(2)]
        WHS = [AR.at("s", f"WHS{i}", [128, HF], BF16) for i in range(2)]
        WLS = [AR.at("s", f"WLS{i}", [128, HF], BF16) for i in range(2)]
        WSTR = [AR.at("s", f"WSTR{i}", [128, D], F32) for i in range(2)]
        WQT = AR.at("s", "WQT", [128, D], F32)
        E1SB = AR.at("s", "E1SB", [16, D], F32)
        E2SB = AR.at("s", "E2SB", [16, D], F32)
        POOL = AR.at("s", "POOL", [16, D], F32)
        S1COL = AR.at("s", "S1COL", [16, cfg.nT], F32)
        S2COL = AR.at("s", "S2COL", [16, cfg.nT], F32)
        E1T = AR.at("s", "E1T", [128, 128], F32)
        ETM = [AR.at("s", f"ETM{i}", [128, 16], F32) for i in range(2)]
        EHM = [AR.at("s", f"EHM{i}", [16, 128], F32) for i in range(2)]
        BQROW = AR.at("s", "BQROW", [1, D], F32)
        SXB = [AR.at("s", f"SXB{i}", [128, D], F32) for i in range(4)]
        SXNT = [AR.at("s", f"SXNT{i}", [128, D], F32) for i in range(4)]
        BKROW = AR.at("s", "BKROW", [1, D], F32)
        XNH = AR.at("q", "XNH", [128, NCH * 1024], BF16)
        XNL = AR.at("q", "XNL", [128, NCH * 1024], BF16)
        GACC = AR.at("q", "GACC", [128, 8 * D], F32)
        RWQA = AR.at("q", "RWQA", [128, 2 * NCH * 1024], BF16)
        RWQB = AR.at("q", "RWQB", [128, 2 * 4 * D], BF16)
        H1H = AR.at("q", "H1H", [128, 4 * 512], BF16)
        H1L = AR.at("q", "H1L", [128, 4 * 512], BF16)
        T0 = [AR.at("q", "T00", [128, 512], F32)]
        T0 = [T0[0], T0[0]]
        W8 = AR.at("q", "W8", [128, 8 * 8], F32)
        WT9 = AR.at("q", "WT9", [8, 1024], F32)
        MQHB = AR.at("q", "MQHB", [128, NCH * 128], BF16)
        MQLB = AR.at("q", "MQLB", [128, NCH * 128], BF16)
        WVH = AR.at("q", "WVH", [128, NCH * 128], BF16)
        WVL = AR.at("q", "WVL", [128, NCH * 128], BF16)
        TMQF = [AR.at("q", f"TMQF{i}", [128, 128], F32) for i in range(2)]
        RL = AR.at("q", "RL", [128, 8], F32)
        RP = AR.at("q", "RP", [128, 8], F32)
        RM = AR.at("q", "RM", [128, 2], F32)
        OUTB = AR.at("q", "OUTB", [128, D], F32)
        RB2SB2 = AR.at("q", "RB2SB2", [8, D], F32)
        SB2ROW = AR.at("q", "SB2ROW", [1, D], F32)
        BIAS1 = AR.at("q", "BIAS1", [128, 16 * NE], F32)

        PS = [es.enter_context(nc.psum_tensor(f"PS{i}", [128, 1024], F32))
              for i in range(4)]

        P = Prog(nc, es)
        for s in ("st", "pe", "dv", "sc", "gp"):
            P.sem(s)
        block = es.enter_context(nc.Block())

        def dma(eng, out, in_, reads=(), writes=(), extra=(), sem=None,
                slow=False, wguard=True):
            if sem is None:
                key = writes[0] if writes else (reads[0] if reads else "misc")
                sem = "D_" + key
            if slow:
                fn = lambda E, o=out, i=in_: E.dma_start(
                    out=o, in_=i, allow_slow_non_contiguous=True)
            else:
                fn = lambda E, o=out, i=in_: E.dma_start(out=o, in_=i)
            return P.op(eng, fn, reads=reads, writes=writes, extra=extra,
                        inc=sem, k=16, wguard=wguard)

        P.op("gpsimd", lambda E: E.iota(
            out=ID[:], pattern=[[1, 128]], base=0, channel_multiplier=0,
            allow_small_or_imprecise_dtypes=True), writes=["ID"], inc="gp")
        P.op("gpsimd", lambda E: E.iota(
            out=PIV[:], pattern=[[0, 1]], base=0, channel_multiplier=1,
            allow_small_or_imprecise_dtypes=True), writes=["PIV"], inc="gp")
        P.op("vector", lambda E: E.tensor_scalar(
            out=ID[:], in0=ID[:], scalar1=PIV[:], scalar2=None,
            op0=ALU.is_equal), reads=["ID", "PIV"], writes=["ID"], inc="dv")
        P.drain("vector")
        P.op("vector", lambda E: E.tensor_copy(out=IDH[:], in_=ID[:]),
             reads=["ID"], writes=["IDH"], inc="dv")
        P.op("vector", lambda E: E.memset(CONSTS[:, 0:1], 1e-5),
             writes=["CONSTS"])
        P.op("vector", lambda E: E.memset(ONES1[:], 1.0), writes=["ONES1"])
        P.op("vector", lambda E: E.memset(ONESH[:], 1.0), writes=["ONESH"],
             inc="dv")

        dma("sync", BQ8[:], d_bq.rearrange("(c p) -> p c", p=128),
            writes=["BQ8"], slow=True)
        dma("sync", BK8[:], d_bk.rearrange("(c p) -> p c", p=128),
            writes=["BK8"], slow=True)
        dma("sync", BQROW[:], d_bq.rearrange("(o d) -> o d", o=1), writes=["BQROW"])
        dma("sync", BKROW[:], d_bk.rearrange("(o d) -> o d", o=1), writes=["BKROW"])
        dma("sync", BQAROW[:], d_bqa.rearrange("(o d) -> o d", o=1), writes=["BQAROW"])
        dma("sync", BKAROW[:], d_bka.rearrange("(o d) -> o d", o=1), writes=["BKAROW"])
        dma("sync", RBROW[:], d_rwb.rearrange("(o d) -> o d", o=1), writes=["RBROW"])
        for c in range(NCH):
            dma("sync", Wqa8[:, 16 * c:16 * (c + 1)],
                d_wqa[128 * c:128 * (c + 1), :], writes=["Wqa8"])
            dma("sync", Wka8[:, 16 * c:16 * (c + 1)],
                d_wka[128 * c:128 * (c + 1), :], writes=["Wka8"])
        def barrier():
            mpe = P.drain("tensor", inc="pe")
            mdv = P.drain("vector", inc="dv")
            msc = P.drain("scalar", inc="sc")
            mgp = P.drain("gpsimd", inc="gp")
            waits = [("pe", mpe), ("dv", mdv), ("sc", msc), ("gp", mgp)]
            waits += [(nm, c) for nm, c in P.cnt.items()
                      if (nm.startswith("D_") or nm == "st") and c > 0]
            for eng in ENGS:
                P._raw(eng, lambda E: E.nop(), waits=waits, inc=None, k=1)

        for k in range(NCH):
            dma("sync", WSTR[k % 2][:], d_wq[128 * k:128 * (k + 1), :],
                writes=[f"WSTR{k % 2}"])
            for j in range(NCH):
                mk = P.op("tensor", lambda E, k=k, j=j: E.transpose(
                    out=PS[2][:, 128 * j:128 * (j + 1)],
                    in_=WSTR[k % 2][:, 128 * j:128 * (j + 1)], identity=ID[:]),
                    reads=[f"WSTR{k % 2}", "ID"],
                    writes=["PS2A", "PS2B"],
                    inc="pe" if j == NCH - 1 else None)
            P.op("vector", lambda E: E.tensor_copy(out=WQT[:], in_=PS[2][:]),
                 reads=["PS2A", "PS2B"], writes=["WQT"], inc="dv")
            for j in range(NCH):
                mk = P.op("tensor", lambda E, k=k, j=j: E.matmul(
                    out=PS[1][0:128, 0:16], lhsT=WQT[:, 128 * j:128 * (j + 1)],
                    rhs=Wqa8[:, 16 * j:16 * (j + 1)],
                    start=(j == 0), stop=(j == NCH - 1)),
                    reads=["WQT", "Wqa8"], writes=["PS1A"],
                    inc="pe" if j == NCH - 1 else None)
            P.op("vector", lambda E, k=k: E.tensor_copy(
                out=A8[:, 16 * k:16 * (k + 1)], in_=PS[1][0:128, 0:16]),
                reads=["PS1A"], writes=["A8"], inc="dv")
            b2 = k % 2
            P.op("scalar", lambda E, k=k, b2=b2: E.copy(
                out=WHS[b2][:, 0:D], in_=WSTR[b2][:]),
                reads=[f"WSTR{b2}"], writes=[f"WHS{b2}"], inc="sc")
            P.op("vector", lambda E, k=k, b2=b2: E.tensor_tensor(
                out=WLS[b2][:, 0:D], in0=WSTR[b2][:],
                in1=WHS[b2][:, 0:D], op=ALU.subtract),
                reads=[f"WSTR{b2}", f"WHS{b2}"], writes=[f"WLS{b2}"], inc="dv")
            dma("gpsimd", d_wqh[k], WHS[b2][:, 0:D], reads=[f"WHS{b2}"],
                writes=["d_wqh"], sem=f"D_spl{b2}")
            dma("gpsimd", d_wql[k], WLS[b2][:, 0:D],
                reads=[f"WLS{b2}", f"WHS{b2}"],
                writes=["d_wql", "d_wqh"], sem=f"D_spl{b2}")
        for j in range(NCH):
            P.op("tensor", lambda E, j=j: E.matmul(
                out=PS[1][0:1, 512:528], lhsT=BQ8[:, j:j + 1],
                rhs=Wqa8[:, 16 * j:16 * (j + 1)],
                start=(j == 0), stop=(j == NCH - 1)),
                reads=["BQ8", "Wqa8"], writes=["PS1B"],
                inc="pe" if j == NCH - 1 else None)
        P.op("vector", lambda E: E.tensor_tensor(
            out=TROW[:], in0=PS[1][0:1, 512:528], in1=BQAROW[:], op=ALU.add),
            reads=["PS1B", "BQAROW"], writes=["TROW"], inc="dv")
        P.op("tensor", lambda E: E.transpose(
            out=PS[1][0:16, 528:529], in_=TROW[:], identity=ID[0:1, 0:1]),
            reads=["TROW", "ID"], writes=["PS1B"], inc="pe")
        P.op("vector", lambda E: E.tensor_scalar(
            out=A0S[:], in0=PS[1][0:16, 528:529], scalar1=0.125, scalar2=None,
            op0=ALU.mult), reads=["PS1B"], writes=["A0S"], inc="dv")

        split_jobs = []
        for k in range(NCH):
            split_jobs.append(("res", d_wt[128 * k:128 * (k + 1), :], D,
                               WTH, WTL, D * k, "WTH", "WTL"))
        split_jobs.append(("resm", [d_rww[128 * c:128 * (c + 1), :]
                                    for c in range(NCH)],
                           64, RWH8, RWL8, 0, "RWH8", "RWL8"))
        for e in range(NE):
            s1 = d_sw1 if e == 8 else d_rw1[e]
            for c in range(NCH):
                split_jobs.append(("dram", s1[128 * c:128 * (c + 1), :], HF,
                                   d_w1h[e, c], d_w1l[e, c], None, "w1h", "w1l"))
            s2 = d_sw2 if e == 8 else d_rw2[e]
            for c in range(HF // 128):
                split_jobs.append(("dram", s2[128 * c:128 * (c + 1), :], D,
                                   d_w2h[e, c], d_w2l[e, c], None, "w2h", "w2l"))

        def emit_split(u):
            kind, src, width, dh, dl, off, kh, kl = split_jobs[u]
            b = u % 2
            if kind == "resm":
                w0 = width // len(src)
                for ci, sap in enumerate(src):
                    dma("sync", WSTG[b][:, w0 * ci:w0 * (ci + 1)], sap,
                        writes=[f"WSTG{b}"])
                kind = "res"
            else:
                dma("sync", WSTG[b][:, 0:width], src, writes=[f"WSTG{b}"])
            if kind == "res":
                P.op("scalar", lambda E: E.copy(
                    out=dh[:, off:off + width], in_=WSTG[b][:, 0:width]),
                    reads=[f"WSTG{b}"], writes=[kh], inc="sc")
                P.op("vector", lambda E: E.tensor_tensor(
                    out=dl[:, off:off + width], in0=WSTG[b][:, 0:width],
                    in1=dh[:, off:off + width], op=ALU.subtract),
                    reads=[f"WSTG{b}", kh], writes=[kl], inc="dv")
            else:
                P.op("scalar", lambda E: E.copy(
                    out=WHS[b][:, 0:width], in_=WSTG[b][:, 0:width]),
                    reads=[f"WSTG{b}"], writes=[f"WHS{b}"], inc="sc")
                subeng, subsem = (("vector", "dv") if (u // 2) % 2 == 0
                                  else ("gpsimd", "gp"))
                P.op(subeng, lambda E: E.tensor_tensor(
                    out=WLS[b][:, 0:width], in0=WSTG[b][:, 0:width],
                    in1=WHS[b][:, 0:width], op=ALU.subtract),
                    reads=[f"WSTG{b}", f"WHS{b}"], writes=[f"WLS{b}"],
                    inc=subsem)
                dma("gpsimd", dh, WHS[b][:, 0:width], reads=[f"WHS{b}"],
                    writes=[kh], sem=f"D_spl{b}")
                dma("gpsimd", dl, WLS[b][:, 0:width],
                    reads=[f"WLS{b}", f"WHS{b}"],
                    writes=[kl, kh], sem=f"D_spl{b}")

        def emit_ln(src_ap, xkey, xbuf, dkey, dst, par=0):
            SL = SML[:, 8 * par:8 * par + 8]
            skey = f"SML{par}"
            dma("sync", xbuf[:], src_ap, writes=[xkey])
            P.op("vector", lambda E: E.tensor_reduce(
                out=SL[:, 0:1], in_=xbuf[:], axis=AX.X, op=ALU.add),
                reads=[xkey], writes=[skey], inc="dv")
            P.op("scalar", lambda E: E.activation(
                out=dst[:], in_=xbuf[:], func=AF.Square, bias=0.0, scale=1.0,
                accum_out=SL[:, 4:5]),
                reads=[xkey], writes=[dkey, skey], inc="sc")
            P.op("vector", lambda E: E.tensor_scalar(
                out=SL[:, 1:2], in0=SL[:, 0:1], scalar1=1.0 / D,
                scalar2=None, op0=ALU.mult), reads=[skey], writes=[skey])
            P.drain("vector")
            P.op("vector", lambda E: E.tensor_tensor(
                out=SL[:, 2:3], in0=SL[:, 1:2], in1=SL[:, 1:2],
                op=ALU.mult))
            P.op("vector", lambda E: E.tensor_scalar(
                out=SL[:, 3:4], in0=SL[:, 4:5], scalar1=1.0 / D,
                scalar2=None, op0=ALU.mult))
            P.drain("vector")
            P.op("vector", lambda E: E.tensor_tensor(
                out=SL[:, 3:4], in0=SL[:, 3:4], in1=SL[:, 2:3],
                op=ALU.subtract), writes=[skey], inc="dv")
            P.op("scalar", lambda E: E.activation(
                out=SL[:, 5:6], in_=SL[:, 3:4], func=AF.Sqrt, bias=1e-5,
                scale=1.0), reads=[skey], writes=[skey], inc="sc")
            P.op("vector", lambda E: E.reciprocal(
                out=SL[:, 6:7], in_=SL[:, 5:6]),
                reads=[skey], writes=[skey])
            P.drain("vector")
            m = P.op("vector", lambda E: E.tensor_scalar(
                out=dst[:], in0=xbuf[:], scalar1=SL[:, 1:2],
                scalar2=SL[:, 6:7], op0=ALU.subtract, op1=ALU.mult),
                reads=[xkey, skey], writes=[dkey], inc="dv")
            return m

        def emit_spine(pass1):
            def emit_tail(t):
                tb4 = t % 4
                tb = t % 2
                qo = 512 * (t % 2)
                rkey = "PS1A" if t % 2 == 0 else "PS1B"
                P.op("tensor", lambda E, tb=tb, qo=qo: E.transpose(
                    out=PS[1][0:128, qo + 128:qo + 144], in_=EHM[tb][:],
                    identity=ID[0:16, 0:16]),
                    reads=[f"EHM{tb}", "ID"], writes=[rkey], inc="pe")
                P.op("vector", lambda E, tb=tb, qo=qo: E.tensor_copy(
                    out=ETM[tb][:], in_=PS[1][0:128, qo + 128:qo + 144]),
                    reads=[rkey], writes=[f"ETM{tb}"], inc="dv")
                for h in range(2):
                    P.op("tensor", lambda E, tb4=tb4, tb=tb, h=h, t=t: E.matmul(
                        out=PS[0][0:16, 512 * h:512 * (h + 1)], lhsT=ETM[tb][:],
                        rhs=SXNT[tb4][:, 512 * h:512 * (h + 1)],
                        start=(t == 0), stop=(t == cfg.nT - 1)),
                        reads=[f"ETM{tb}", f"SXNT{tb4}"],
                        writes=["PS0A" if h == 0 else "PS0B"],
                        inc="pe" if h == 1 else None)

            AB = A8 if pass1 else C8
            BSC = A0S if pass1 else C0S
            SCOL = S1COL if pass1 else S2COL
            for t in range(cfg.nT):
                g, tb4, tb, gb = t // 4, t % 4, t % 2, (t // 4) % 2
                c0 = 128 * (t % 4)
                qo = 512 * (t % 2)
                if pass1:
                    emit_ln(d_xb[128 * t:128 * (t + 1), :], f"SXB{tb4}",
                            SXB[tb4], f"SXNT{tb4}", SXNT[tb4], par=tb4)
                    dma("gpsimd", d_xtm[128 * t:128 * (t + 1), :],
                        SXNT[tb4][:], reads=[f"SXNT{tb4}"], writes=["xtm"],
                        sem=f"D_xtm{tb}")
                    pst = PS[2 + (t % 2)]
                    pkeys = [f"PS{2 + (t % 2)}A", f"PS{2 + (t % 2)}B"]
                    for j in range(NCH):
                        P.op("tensor", lambda E, j=j, pst=pst, tb4=tb4:
                             E.transpose(
                                 out=pst[:, 128 * j:128 * (j + 1)],
                                 in_=SXNT[tb4][:, 128 * j:128 * (j + 1)],
                                 identity=ID[:]),
                             reads=[f"SXNT{tb4}", "ID"], writes=pkeys,
                             inc="pe" if j == NCH - 1 else None)
                    for j in range(NCH):
                        P.op("scalar", lambda E, j=j, pst=pst, gb=gb, c0=c0:
                             E.copy(out=XNFM[gb][:, 512 * j + c0:
                                                 512 * j + c0 + 128],
                                    in_=pst[:, 128 * j:128 * (j + 1)]),
                             reads=pkeys, writes=[f"XNFM{gb}"],
                             inc="sc" if j == NCH - 1 else None)
                else:
                    if t % 4 == 0:
                        for j in range(NCH):
                            dma("gpsimd", XNFM[gb][:, 512 * j:512 * (j + 1)],
                                d_xfm[j, :, 512 * g:512 * (g + 1)],
                                reads=["xfm"], writes=[f"XNFM{gb}"],
                                wguard=(j == 0))
                    dma("sync", SXNT[tb4][:], d_xtm[128 * t:128 * (t + 1), :],
                        reads=["xtm"], writes=[f"SXNT{tb4}"])
                rkey = "PS1A" if t % 2 == 0 else "PS1B"
                for j in range(NCH):
                    P.op("tensor", lambda E, j=j, qo=qo, gb=gb, c0=c0, AB=AB:
                         E.matmul(
                             out=PS[1][0:16, qo:qo + 128],
                             lhsT=AB[:, 16 * j:16 * (j + 1)],
                             rhs=XNFM[gb][:, 512 * j + c0:512 * j + c0 + 128],
                             start=(j == 0), stop=(j == NCH - 1)),
                         reads=[f"XNFM{gb}", "A8" if pass1 else "C8"],
                         writes=[rkey],
                         inc="pe" if j == NCH - 1 else None)
                P.op("scalar", lambda E, t=t, tb=tb, qo=qo, BSC=BSC, SCOL=SCOL:
                     E.activation(out=EHM[tb][:], in_=PS[1][0:16, qo:qo + 128],
                                  func=AF.Exp, bias=BSC[:], scale=0.125,
                                  accum_out=SCOL[:, t:t + 1]),
                     reads=[rkey, "A0S" if pass1 else "C0S"],
                     writes=[f"EHM{tb}", "S1COL" if pass1 else "S2COL"],
                     inc="sc")
                if t >= 1:
                    emit_tail(t - 1)
                if pass1:
                    if t % 4 == 3:
                        for j in range(NCH):
                            dma("gpsimd", d_xfm[j, :, 512 * g:512 * (g + 1)],
                                XNFM[gb][:, 512 * j:512 * (j + 1)],
                                reads=[f"XNFM{gb}"], writes=["xfm"],
                                sem=f"D_xfm{gb}", wguard=(j == 0))
                base = 0 if pass1 else (len(split_jobs) * 55) // 100
                top = (len(split_jobs) * 55) // 100 if pass1 else len(split_jobs)
                quota = base + ((t + 1) * (top - base)) // cfg.nT
                while emit_spine.spl_u < quota:
                    emit_split(emit_spine.spl_u)
                    emit_spine.spl_u += 1
            emit_tail(cfg.nT - 1)
            SR = S1R if pass1 else S2R
            SREC = S1REC if pass1 else S2REC
            ESB = E1SB if pass1 else E2SB
            P.op("vector", lambda E, SR=SR, SCOL=SCOL: E.tensor_reduce(
                out=SR[:], in_=SCOL[:], axis=AX.X, op=ALU.add),
                reads=["S1COL" if pass1 else "S2COL"],
                writes=["S1R" if pass1 else "S2R"], inc="dv")
            P.drain("vector")
            P.op("vector", lambda E, SR=SR, SREC=SREC: E.reciprocal(
                out=SREC[:], in_=SR[:]),
                reads=["S1R" if pass1 else "S2R"],
                writes=["S1REC" if pass1 else "S2REC"], inc="dv")
            P.op("vector", lambda E, ESB=ESB: E.tensor_copy(
                out=ESB[:], in_=PS[0][0:16, :]),
                reads=["PS0A", "PS0B"], writes=["E1SB" if pass1 else "E2SB"],
                inc="dv")

        emit_spine.spl_u = 0
        emit_spine(True)

        def emit_pool(tag, ESBk, ESB, SR, SRECk, SREC, brow_key, brow, PX_key,
                      PX, mul_pq):
            skey = "S1R" if tag == "q" else "S2R"
            P.op("tensor", lambda E: E.transpose(
                out=PS[1][0:1, 0:16], in_=SR[:], identity=ID[0:16, 0:16]),
                reads=[skey, "ID"], writes=["PS1A"], inc="pe")
            P.op("vector", lambda E: E.tensor_copy(
                out=SROW[:], in_=PS[1][0:1, 0:16]),
                reads=["PS1A"], writes=["SROW"], inc="dv")
            for k in range(NCH):
                P.op("tensor", lambda E, k=k: E.transpose(
                    out=PS[1][0:128, 16 * k + 32:16 * k + 48],
                    in_=ESB[:, 128 * k:128 * (k + 1)],
                    identity=ID[0:16, 0:16]),
                    reads=[ESBk, "ID"], writes=["PS1A"], inc="pe")
                P.op("vector", lambda E, k=k: E.tensor_copy(
                    out=E1T[:, 16 * k:16 * (k + 1)],
                    in_=PS[1][0:128, 16 * k + 32:16 * k + 48]),
                    reads=["PS1A"], writes=["E1T"], inc="dv")
            wsrc = d_wq if tag == "q" else d_wk
            for k in range(NCH):
                dma("sync", WSTR[k % 2][:], wsrc[128 * k:128 * (k + 1), :],
                    writes=[f"WSTR{k % 2}"])
                for h in range(2):
                    P.op("tensor", lambda E, k=k, h=h: E.matmul(
                        out=PS[0][0:16, 512 * h:512 * (h + 1)],
                        lhsT=E1T[:, 16 * k:16 * (k + 1)],
                        rhs=WSTR[k % 2][:, 512 * h:512 * (h + 1)],
                        start=(k == 0), stop=False),
                        reads=["E1T", f"WSTR{k % 2}"],
                        writes=["PS0A" if h == 0 else "PS0B"],
                        inc="pe" if h == 1 else None)
            for h in range(2):
                P.op("tensor", lambda E, h=h: E.matmul(
                    out=PS[0][0:16, 512 * h:512 * (h + 1)], lhsT=SROW[:],
                    rhs=brow[0:1, 512 * h:512 * (h + 1)],
                    start=False, stop=True),
                    reads=["SROW", brow_key],
                    writes=["PS0A" if h == 0 else "PS0B"],
                    inc="pe" if h == 1 else None)
            P.op("vector", lambda E: E.tensor_scalar(
                out=POOL[:], in0=PS[0][0:16, :], scalar1=SREC[:],
                scalar2=None, op0=ALU.mult),
                reads=["PS0A", "PS0B", SRECk], writes=["POOL"], inc="dv")
            for k in range(NCH):
                P.op("tensor", lambda E, k=k: E.transpose(
                    out=PS[1][0:128, 16 * k + 64:16 * k + 80],
                    in_=POOL[:, 128 * k:128 * (k + 1)],
                    identity=ID[0:16, 0:16]),
                    reads=["POOL", "ID"], writes=["PS1B"], inc="pe")
                P.op("vector", lambda E, k=k: E.tensor_copy(
                    out=PX[0:64, k:k + 1],
                    in_=PS[1][0:64, 16 * k + 64 + 2 * k:16 * k + 65 + 2 * k]),
                    reads=["PS1B"], writes=[PX_key])
                P.op("vector", lambda E, k=k: E.tensor_copy(
                    out=PX[64:128, k:k + 1],
                    in_=PS[1][64:128, 16 * k + 65 + 2 * k:16 * k + 66 + 2 * k]),
                    reads=["PS1B"], writes=[PX_key], inc="dv")
            if mul_pq:
                P.op("vector", lambda E: E.tensor_tensor(
                    out=PX[:], in0=PX[:], in1=PQ8[:], op=ALU.mult),
                    reads=[PX_key, "PQ8"], writes=[PX_key], inc="dv")

        emit_pool("q", "E1SB", E1SB, S1R, "S1REC", S1REC, "BQROW", BQROW,
                  "PQ8", PQ8, False)

        for k in range(NCH):
            P.op("vector", lambda E, k=k: E.tensor_scalar(
                out=WKA2[:, 16 * k:16 * (k + 1)],
                in0=Wka8[:, 16 * k:16 * (k + 1)],
                scalar1=PQ8[:, k:k + 1], scalar2=None, op0=ALU.mult),
                reads=["Wka8", "PQ8"], writes=["WKA2"],
                inc="dv" if k == NCH - 1 else None)
        for k in range(NCH):
            dma("sync", WSTR[k % 2][:], d_wk[128 * k:128 * (k + 1), :],
                writes=[f"WSTR{k % 2}"])
            for j in range(NCH):
                P.op("tensor", lambda E, k=k, j=j: E.transpose(
                    out=PS[2][:, 128 * j:128 * (j + 1)],
                    in_=WSTR[k % 2][:, 128 * j:128 * (j + 1)], identity=ID[:]),
                    reads=[f"WSTR{k % 2}", "ID"], writes=["PS2A", "PS2B"],
                    inc="pe" if j == NCH - 1 else None)
            P.op("vector", lambda E: E.tensor_copy(out=WQT[:], in_=PS[2][:]),
                 reads=["PS2A", "PS2B"], writes=["WQT"], inc="dv")
            for j in range(NCH):
                P.op("tensor", lambda E, k=k, j=j: E.matmul(
                    out=PS[1][0:128, 0:16], lhsT=WQT[:, 128 * j:128 * (j + 1)],
                    rhs=WKA2[:, 16 * j:16 * (j + 1)],
                    start=(j == 0), stop=(j == NCH - 1)),
                    reads=["WQT", "WKA2"], writes=["PS1A"],
                    inc="pe" if j == NCH - 1 else None)
            P.op("vector", lambda E, k=k: E.tensor_copy(
                out=C8[:, 16 * k:16 * (k + 1)], in_=PS[1][0:128, 0:16]),
                reads=["PS1A"], writes=["C8"], inc="dv")
        for j in range(NCH):
            P.op("tensor", lambda E, j=j: E.matmul(
                out=PS[1][0:1, 512:528], lhsT=BK8[:, j:j + 1],
                rhs=WKA2[:, 16 * j:16 * (j + 1)],
                start=(j == 0), stop=(j == NCH - 1)),
                reads=["BK8", "WKA2"], writes=["PS1B"],
                inc="pe" if j == NCH - 1 else None)
        P.op("vector", lambda E: E.tensor_tensor(
            out=TROW[:], in0=PS[1][0:1, 512:528], in1=BKAROW[:], op=ALU.add),
            reads=["PS1B", "BKAROW"], writes=["TROW"], inc="dv")
        P.op("tensor", lambda E: E.transpose(
            out=PS[1][0:16, 528:529], in_=TROW[:], identity=ID[0:1, 0:1]),
            reads=["TROW", "ID"], writes=["PS1B"], inc="pe")
        P.op("vector", lambda E: E.tensor_scalar(
            out=C0S[:], in0=PS[1][0:16, 528:529], scalar1=0.125, scalar2=None,
            op0=ALU.mult), reads=["PS1B"], writes=["C0S"], inc="dv")

        emit_spine(False)
        emit_pool("k", "E2SB", E2SB, S2R, "S2REC", S2REC, "BKROW", BKROW,
                  "PK8", PK8, True)

        if cfg.debug:
            dma("gpsimd", dbg["dbg_pq"], PQ8[:], reads=["PQ8"], sem="st")
            dma("gpsimd", dbg["dbg_pk"], PK8[:], reads=["PK8"], sem="st")
            dma("gpsimd", dbg["dbg_s1"], S1R[:], reads=["S1R"], sem="st")
            dma("gpsimd", dbg["dbg_s2"], S2R[:], reads=["S2R"], sem="st")
            dma("gpsimd", dbg["dbg_e1"], E1SB[:], reads=["E1SB"], sem="st")
            dma("gpsimd", dbg["dbg_e2"], E2SB[:], reads=["E2SB"], sem="st")
            dma("gpsimd", dbg["dbg_a"], A8[:], reads=["A8"], sem="st")
            dma("gpsimd", dbg["dbg_c"], C8[:], reads=["C8"], sem="st")

        barrier()
        dma("sync", RB2SB2[0:8, :], d_rb2, writes=["RB2SB2"])
        dma("sync", SB2ROW[:], d_sb2.rearrange("(o d) -> o d", o=1),
            writes=["SB2ROW"])
        for e in range(8):
            dma("sync", BIAS1[:, 16 * e:16 * e + 16],
                d_rb1[e].rearrange("(c p) -> p c", p=128),
                writes=["BIAS1"], slow=True)
        dma("sync", BIAS1[:, 16 * 8:16 * 9],
            d_sb1.rearrange("(c p) -> p c", p=128), writes=["BIAS1"], slow=True)
        dma("sync", OUTB[0:1, :], d_bt.rearrange("(o d) -> o d", o=1), writes=["OUTB"])
        P.op("vector", lambda E: E.tensor_copy(out=BTROWH[:], in_=OUTB[0:1, :]),
             reads=["OUTB"], writes=["BTROWH"], inc="dv")
        P.drain("vector")
        P.op("vector", lambda E: E.tensor_tensor(
            out=BTROWL[:], in0=OUTB[0:1, :], in1=BTROWH[:], op=ALU.subtract),
            reads=["OUTB", "BTROWH"], writes=["BTROWL"], inc="dv")

        HB = 4
        NHQ = HF // 512
        e_order = [8] + list(range(8))

        for q in range(cfg.NQ):
            q0 = q * cfg.Q

            for tb in range(cfg.tbs):
                t0c = 128 * tb
                b = tb % 2
                emit_ln(d_xo[q0 + t0c:q0 + t0c + 128, :], f"XB{b}", XB[b],
                        f"XNT{b}", XNT[b], par=b)
                pst = PS[2]
                pkeys = ["PS2A", "PS2B"]
                for j in range(NCH):
                    P.op("tensor", lambda E, j=j, pst=pst, b=b: E.transpose(
                        out=pst[:, 128 * j:128 * (j + 1)],
                        in_=XNT[b][:, 128 * j:128 * (j + 1)], identity=ID[:]),
                        reads=[f"XNT{b}", "ID"], writes=pkeys,
                        inc="pe" if j == NCH - 1 else None)
                for j in range(NCH):
                    P.op("scalar", lambda E, j=j, pst=pst, t0c=t0c: E.copy(
                        out=XNH[:, 1024 * j + t0c:1024 * j + t0c + 128],
                        in_=pst[:, 128 * j:128 * (j + 1)]),
                        reads=pkeys, writes=["XNH"],
                        inc="sc" if j == NCH - 1 else None)
                for j in range(NCH):
                    P.op("vector", lambda E, j=j, pst=pst, t0c=t0c:
                         E.tensor_tensor(
                             out=XNL[:, 1024 * j + t0c:1024 * j + t0c + 128],
                             in0=pst[:, 128 * j:128 * (j + 1)],
                             in1=XNH[:, 1024 * j + t0c:1024 * j + t0c + 128],
                             op=ALU.subtract),
                         reads=pkeys + ["XNH"], writes=["XNL"],
                         inc="dv" if j == NCH - 1 else None)
                P.op("tensor", lambda E: E.matmul(
                    out=PS[3][0:128, 0:8], lhsT=ONES1[:], rhs=RBROW[:],
                    start=True, stop=False),
                    reads=["ONES1", "RBROW"], writes=["PS3A"])
                for j in range(NCH):
                    for pi, (xa, wa, xk, wk_) in enumerate((
                            (XNH, RWH8, "XNH", "RWH8"),
                            (XNH, RWL8, "XNH", "RWL8"),
                            (XNL, RWH8, "XNL", "RWH8"))):
                        P.op("tensor", lambda E, j=j, xa=xa, wa=wa, t0c=t0c,
                             sp=(j == NCH - 1 and pi == 2):
                             E.matmul(
                                 out=PS[3][0:128, 0:8],
                                 lhsT=xa[:, 1024 * j + t0c:1024 * j + t0c + 128],
                                 rhs=wa[:, 8 * j:8 * (j + 1)],
                                 start=False, stop=sp),
                             reads=[xk, wk_], writes=["PS3A"],
                             inc="pe" if (j == NCH - 1 and pi == 2) else None)
                P.op("vector", lambda E: E.tensor_copy(
                    out=RL[:], in_=PS[3][0:128, 0:8]),
                    reads=["PS3A"], writes=["RL"], inc="dv")
                P.drain("vector")
                P.op("vector", lambda E: E.reduce_max(
                    out=RM[:, 0:1], in_=RL[:], axis=AX.X))
                P.drain("vector")
                P.op("vector", lambda E: E.tensor_scalar(
                    out=RP[:], in0=RL[:], scalar1=RM[:, 0:1], scalar2=None,
                    op0=ALU.is_equal))
                P.drain("vector")
                P.op("vector", lambda E: E.scalar_tensor_tensor(
                    out=RP[:], in0=RP[:], scalar=-1e30, in1=RL[:],
                    op0=ALU.mult, op1=ALU.add))
                P.drain("vector")
                P.op("vector", lambda E: E.reduce_max(
                    out=RM[:, 1:2], in_=RP[:], axis=AX.X, negate=True))
                P.drain("vector")
                P.op("vector", lambda E: E.tensor_scalar(
                    out=RP[:], in0=RL[:], scalar1=RM[:, 1:2], scalar2=0.0,
                    op0=ALU.add, op1=ALU.is_ge))
                P.drain("vector")
                P.op("vector", lambda E: E.reduce_max(
                    out=RM[:, 0:1], in_=RL[:], axis=AX.X, negate=True),
                    writes=["RM"], reads=["RL"], inc="dv")
                P.op("scalar", lambda E: E.activation(
                    out=RL[:], in_=RL[:], func=AF.Exp, bias=RM[:, 0:1],
                    scale=1.0, accum_out=RM[:, 1:2]),
                    reads=["RL", "RM"], writes=["RL", "RM"], inc="sc")
                P.op("vector", lambda E: E.tensor_tensor(
                    out=RP[:], in0=RP[:], in1=RL[:], op=ALU.mult),
                    reads=["RP", "RL"], writes=["RP"])
                P.drain("vector")
                P.op("vector", lambda E: E.reciprocal(
                    out=RM[:, 0:1], in_=RM[:, 1:2]), reads=["RM"],
                    writes=["RM"])
                P.drain("vector")
                P.op("vector", lambda E, tb=tb: E.tensor_scalar(
                    out=W8[:, 8 * tb:8 * (tb + 1)], in0=RP[:],
                    scalar1=RM[:, 0:1], scalar2=None, op0=ALU.mult),
                    reads=["RP", "RM"], writes=["W8"], inc="dv")
                P.op("tensor", lambda E, tb=tb: E.transpose(
                    out=PS[3][0:8, 512:640], in_=W8[:, 8 * tb:8 * (tb + 1)],
                    identity=ID[:]),
                    reads=["W8", "ID"], writes=["PS3B"], inc="pe")
                P.op("vector", lambda E, tb=tb: E.tensor_copy(
                    out=WT9[:, 128 * tb:128 * (tb + 1)],
                    in_=PS[3][0:8, 512:640]),
                    reads=["PS3B"], writes=["WT9"], inc="dv")

            for ei, e in enumerate(e_order):
                for hq in range(NHQ):
                    if hq % 2 == 0:
                        for c in range(NCH):
                            dma("sync", RWQA[:, 1024 * c:1024 * (c + 1)],
                                d_w1h[e, c, :, 1024 * (hq // 2):
                                      1024 * (hq // 2) + 1024],
                                reads=["w1h"], writes=["RWQA"],
                                wguard=(c == 0))
                            dma("sync",
                                RWQA[:, 8192 + 1024 * c:8192 + 1024 * (c + 1)],
                                d_w1l[e, c, :, 1024 * (hq // 2):
                                      1024 * (hq // 2) + 1024],
                                reads=["w1l"], writes=["RWQA"],
                                wguard=False)
                    for c in range(4):
                        dma("sync", RWQB[:, 1024 * c:1024 * (c + 1)],
                            d_w2h[e, 4 * hq + c], reads=["w2h"],
                            writes=["RWQB"], wguard=(c == 0))
                        dma("sync", RWQB[:, 4096 + 1024 * c:4096 + 1024 * (c + 1)],
                            d_w2l[e, 4 * hq + c], reads=["w2l"],
                            writes=["RWQB"], wguard=False)
                    for sg in range(cfg.sgs):
                        for hb in range(HB):
                            hbp = hb % 2
                            reg = PS[0][:, 512 * hbp:512 * hbp + 512]
                            rkey = "PS0A" if hbp == 0 else "PS0B"
                            for j in range(NCH):
                                xsl = slice(1024 * j + 512 * sg,
                                            1024 * j + 512 * sg + 512)
                                wsl = slice(1024 * j + 512 * (hq % 2) + 128 * hb,
                                            1024 * j + 512 * (hq % 2) + 128 * hb
                                            + 128)
                                for pi, (woff, xa, xk) in enumerate((
                                        (0, XNH, "XNH"),
                                        (0, XNL, "XNL"),
                                        (8192, XNH, "XNH"))):
                                    P.op("tensor",
                                         lambda E, woff=woff, xa=xa, wsl=wsl,
                                         xsl=xsl, reg=reg,
                                         st=(j == 0 and pi == 0),
                                         sp=(j == NCH - 1 and pi == 2):
                                         E.matmul(
                                             out=reg,
                                             lhsT=RWQA[:, woff + wsl.start:
                                                       woff + wsl.stop],
                                             rhs=xa[:, xsl],
                                             start=st, stop=sp),
                                         reads=["RWQA", xk], writes=[rkey],
                                         inc="pe" if (j == NCH - 1 and pi == 2)
                                         else None)
                            bcol = 16 * e + 4 * hq + hb
                            P.op("scalar", lambda E, reg=reg, hbp=hbp,
                                 bcol=bcol: E.activation(
                                     out=T0[hbp][:], in_=reg, func=AF.Gelu,
                                     bias=BIAS1[:, bcol:bcol + 1], scale=1.0),
                                 reads=[rkey, "BIAS1"], writes=["T00"],
                                 inc="sc")
                            hsl = slice(512 * hb, 512 * hb + 512)
                            P.op("vector", lambda E, hbp=hbp, hsl=hsl:
                                 E.tensor_copy(out=H1H[:, hsl],
                                               in_=T0[hbp][:]),
                                 reads=["T00"], writes=[f"H1H{hb}"],
                                 inc="dv")
                            P.drain("vector")
                            P.op("vector", lambda E, hbp=hbp, hsl=hsl:
                                 E.tensor_tensor(out=H1L[:, hsl],
                                                 in0=T0[hbp][:],
                                                 in1=H1H[:, hsl],
                                                 op=ALU.subtract),
                                 reads=["T00", f"H1H{hb}"],
                                 writes=[f"H1L{hb}"], inc="dv")
                        first = (ei == 0 and hq == 0)
                        for tb4 in range(4):
                            gtb = 4 * sg + tb4
                            pst = PS[1 + (gtb % 2)]
                            pk = [f"PS{1 + (gtb % 2)}A", f"PS{1 + (gtb % 2)}B"]
                            if first:
                                for h in range(2):
                                    P.op("tensor", lambda E, h=h, gtb=gtb,
                                         pst=pst: E.matmul(
                                             out=pst[:, 512 * h:512 * (h + 1)],
                                             lhsT=WT9[:, 128 * gtb:
                                                      128 * (gtb + 1)],
                                             rhs=RB2SB2[:, 512 * h:
                                                        512 * (h + 1)],
                                             start=True, stop=False),
                                         reads=["WT9", "RB2SB2"], writes=pk)
                                    P.op("tensor", lambda E, h=h, pst=pst:
                                         E.matmul(
                                             out=pst[:, 512 * h:512 * (h + 1)],
                                             lhsT=ONES1[:, 0:128],
                                             rhs=SB2ROW[0:1, 512 * h:
                                                        512 * (h + 1)],
                                             start=False, stop=False),
                                         reads=["ONES1", "SB2ROW"], writes=pk)
                            for hc in range(HB):
                                hsl = slice(512 * hc + 128 * tb4,
                                            512 * hc + 128 * tb4 + 128)
                                for pi, (ha, woff, hk) in enumerate((
                                        (H1H, 0, f"H1H{hc}"),
                                        (H1H, 4096, f"H1H{hc}"),
                                        (H1L, 0, f"H1L{hc}"))):
                                    for h in range(2):
                                        P.op("tensor",
                                             lambda E, ha=ha, woff=woff,
                                             hsl=hsl, h=h, hc=hc, pst=pst,
                                             st=(not first and hc == 0
                                                 and pi == 0),
                                             sp=(hc == HB - 1 and pi == 2):
                                             E.matmul(
                                                 out=pst[:, 512 * h:
                                                         512 * (h + 1)],
                                                 lhsT=ha[:, hsl],
                                                 rhs=RWQB[:, woff + 1024 * hc
                                                          + 512 * h:
                                                          woff + 1024 * hc
                                                          + 512 * h + 512],
                                                 start=st, stop=sp),
                                             reads=[hk, "RWQB"], writes=pk,
                                             inc="pe" if (hc == HB - 1 and
                                                          pi == 2 and h == 1)
                                             else None)
                            gk = f"G{gtb}"
                            if first:
                                P.op("vector", lambda E, pst=pst, gtb=gtb:
                                     E.tensor_copy(
                                         out=GACC[:, D * gtb:D * (gtb + 1)],
                                         in_=pst[:]),
                                     reads=pk, writes=[gk], inc="dv")
                            elif e == 8:
                                P.op("vector", lambda E, pst=pst, gtb=gtb:
                                     E.tensor_tensor(
                                         out=GACC[:, D * gtb:D * (gtb + 1)],
                                         in0=GACC[:, D * gtb:D * (gtb + 1)],
                                         in1=pst[:], op=ALU.add),
                                     reads=pk + [gk], writes=[gk], inc="dv")
                            else:
                                wcol = 8 * gtb + e
                                P.op("vector", lambda E, pst=pst, gtb=gtb,
                                     wcol=wcol: E.scalar_tensor_tensor(
                                         out=GACC[:, D * gtb:D * (gtb + 1)],
                                         in0=pst[:],
                                         scalar=W8[:, wcol:wcol + 1],
                                         in1=GACC[:, D * gtb:D * (gtb + 1)],
                                         op0=ALU.mult, op1=ALU.add),
                                     reads=pk + [gk, "W8"], writes=[gk],
                                     inc="dv")

            for gtb in range(cfg.tbs):
                P.op("scalar", lambda E, gtb=gtb: E.activation(
                    out=GACC[:, D * gtb:D * (gtb + 1)],
                    in_=GACC[:, D * gtb:D * (gtb + 1)],
                    func=AF.Silu, bias=0.0, scale=1.0),
                    reads=[f"G{gtb}"], writes=[f"G{gtb}"], inc="sc")

            for k in range(NCH):
                dma("sync", RWQA[:, 1024 * k:1024 * (k + 1)], d_wqh[k],
                    reads=["d_wqh"], writes=["RWQA"], wguard=(k == 0))
                dma("sync", RWQB[:, 1024 * k:1024 * (k + 1)], d_wql[k],
                    reads=["d_wql"], writes=["RWQB"], wguard=(k == 0))

            for gtb in range(cfg.tbs):
                t0c = 128 * gtb
                for fo in range(NCH):
                    reg = PS[0][:, 512 * (fo // 4) + 128 * (fo % 4):
                                512 * (fo // 4) + 128 * (fo % 4) + 128]
                    rkey = "PS0A" if fo < 4 else "PS0B"
                    for j in range(NCH):
                        xsl = slice(1024 * j + t0c, 1024 * j + t0c + 128)
                        wsl = slice(D * j + 128 * fo, D * j + 128 * fo + 128)
                        for pi, (wa, xa, wk_, xk) in enumerate((
                                (RWQA, XNH, "RWQA", "XNH"),
                                (RWQA, XNL, "RWQA", "XNL"),
                                (RWQB, XNH, "RWQB", "XNH"))):
                            P.op("tensor", lambda E, wa=wa, xa=xa, wsl=wsl,
                                 xsl=xsl, reg=reg,
                                 st=(j == 0 and pi == 0),
                                 sp=(j == NCH - 1 and pi == 2): E.matmul(
                                     out=reg, lhsT=wa[:, wsl], rhs=xa[:, xsl],
                                     start=st, stop=sp),
                                 reads=[wk_, xk], writes=[rkey],
                                 inc="pe" if (j == NCH - 1 and pi == 2)
                                 else None)
                    b = gtb % 2
                    P.op("scalar", lambda E, reg=reg, fo=fo, b=b:
                         E.activation(out=TMQF[b][:], in_=reg,
                                      func=AF.Identity,
                                      bias=BQ8[:, fo:fo + 1], scale=1.0),
                         reads=[rkey, "BQ8"], writes=[f"TMQF{b}"], inc="sc")
                    P.op("vector", lambda E, fo=fo, b=b: E.tensor_copy(
                        out=MQHB[:, 128 * fo:128 * (fo + 1)], in_=TMQF[b][:]),
                        reads=[f"TMQF{b}"], writes=["MQHB"], inc="dv")
                    P.drain("vector")
                    P.op("vector", lambda E, fo=fo, b=b: E.tensor_tensor(
                        out=MQLB[:, 128 * fo:128 * (fo + 1)], in0=TMQF[b][:],
                        in1=MQHB[:, 128 * fo:128 * (fo + 1)], op=ALU.subtract),
                        reads=[f"TMQF{b}", "MQHB"], writes=["MQLB"])
                    P.op("vector", lambda E, fo=fo, b=b: E.tensor_scalar(
                        out=WVH[:, 128 * fo:128 * (fo + 1)], in0=TMQF[b][:],
                        scalar1=PK8[:, fo:fo + 1], scalar2=None, op0=ALU.mult),
                        reads=[f"TMQF{b}", "PK8"], writes=["WVH"], inc="dv")
                    P.drain("vector")
                    P.op("vector", lambda E, fo=fo, b=b:
                         E.scalar_tensor_tensor(
                             out=WVL[:, 128 * fo:128 * (fo + 1)],
                             in0=TMQF[b][:], scalar=PK8[:, fo:fo + 1],
                             in1=WVH[:, 128 * fo:128 * (fo + 1)],
                             op0=ALU.mult, op1=ALU.subtract),
                         reads=[f"TMQF{b}", "PK8", "WVH"], writes=["WVL"],
                         inc="dv")
                pst = PS[1 + (gtb % 2)]
                pk = [f"PS{1 + (gtb % 2)}A", f"PS{1 + (gtb % 2)}B"]
                for pi, (bt_, bk_) in enumerate(((BTROWH, "BTROWH"),
                                                 (BTROWL, "BTROWL"))):
                    for h in range(2):
                        P.op("tensor", lambda E, bt_=bt_, h=h, pst=pst,
                             st=(pi == 0): E.matmul(
                                 out=pst[:, 512 * h:512 * (h + 1)],
                                 lhsT=ONESH[:],
                                 rhs=bt_[0:1, 512 * h:512 * (h + 1)],
                                 start=st, stop=False),
                             reads=["ONESH", bk_], writes=pk)
                for j in range(NCH):
                    vsl = slice(128 * j, 128 * (j + 1))
                    for pi, (va, wa, vk, wk_) in enumerate((
                            (WVH, WTH, "WVH", "WTH"),
                            (WVH, WTL, "WVH", "WTL"),
                            (WVL, WTH, "WVL", "WTH"))):
                        for h in range(2):
                            P.op("tensor", lambda E, va=va, wa=wa, vsl=vsl,
                                 j=j, h=h, pst=pst: E.matmul(
                                     out=pst[:, 512 * h:512 * (h + 1)],
                                     lhsT=va[:, vsl],
                                     rhs=wa[:, D * j + 512 * h:
                                            D * j + 512 * h + 512],
                                     start=False, stop=False),
                                 reads=[vk, wk_], writes=pk)
                for pi, (ma_, mkk) in enumerate(((MQHB, "MQHB"),
                                                 (MQLB, "MQLB"))):
                    for j in range(NCH):
                        is_last = (pi == 1 and (j == 3 or j == NCH - 1))
                        P.op("tensor", lambda E, ma_=ma_, j=j, pst=pst,
                             sp=is_last: E.matmul(
                                 out=pst[:, 128 * j:128 * (j + 1)],
                                 lhsT=ma_[:, 128 * j:128 * (j + 1)],
                                 rhs=IDH[:], start=False, stop=sp),
                             reads=[mkk, "IDH"], writes=pk,
                             inc="pe" if (pi == 1 and j == NCH - 1) else None)
                b = gtb % 2
                dma("sync", XB[b][:], d_xo[q0 + t0c:q0 + t0c + 128, :],
                    writes=[f"XB{b}"])
                P.op("vector", lambda E, pst=pst, gtb=gtb: E.tensor_tensor(
                    out=OUTB[:], in0=pst[:], in1=GACC[:, D * gtb:D * (gtb + 1)],
                    op=ALU.mult),
                    reads=pk + [f"G{gtb}"], writes=["OUTB"], inc="dv")
                P.drain("vector")
                P.op("vector", lambda E, b=b: E.tensor_tensor(
                    out=OUTB[:], in0=OUTB[:], in1=XB[b][:], op=ALU.add),
                    reads=["OUTB", f"XB{b}"], writes=["OUTB"], inc="dv")
                dma("gpsimd", d_out[q0 + t0c:q0 + t0c + 128, :], OUTB[:],
                    reads=["OUTB"], sem="st")

        for eng in ("vector", "scalar", "tensor"):
            P.drain(eng, inc={"vector": "dv", "scalar": "sc",
                              "tensor": None}[eng])
        P._raw("gpsimd", lambda E: E.nop(),
               waits=[("st", P.cnt["st"])], inc=None, k=1)
        if getattr(cfg, "verbose", False):
            print("arena views:", AR.cur, "arena size", AR.nbytes,
                  "sbuf remaining", nc.sbuf_bytes_remaining)
        P.replay(block)
    return nc


def kernel(**inputs):
    cfg = Cfg()
    x = np.ascontiguousarray(np.asarray(inputs["L0_attn_hidden_states"],
                                        dtype=np.float32))
    B, S, _ = x.shape
    w = {k: np.ascontiguousarray(np.asarray(v, dtype=np.float32))
         for k, v in inputs.items() if k != "L0_attn_hidden_states"}
    nc = build_nc(cfg)
    wmap = {
        "wq": w["L0_attn_Wq"], "bq": w["L0_attn_bq"],
        "wqa": w["L0_attn_Wqa"], "bqa": w["L0_attn_bqa"],
        "wk": w["L0_attn_Wk"], "bk": w["L0_attn_bk"],
        "wka": w["L0_attn_Wka"], "bka": w["L0_attn_bka"],
        "wt": w["L0_attn_Wt"], "bt": w["L0_attn_bt"],
        "rww": w["L0_ffn_router_w"], "rwb": w["L0_ffn_router_b"],
        "sw1": w["L0_ffn_sw1"], "sb1": w["L0_ffn_sb1"],
        "sw2": w["L0_ffn_sw2"], "sb2": w["L0_ffn_sb2"],
        "rw1": w["L0_ffn_rw1"], "rb1": w["L0_ffn_rb1"],
        "rw2": w["L0_ffn_rw2"], "rb2": w["L0_ffn_rb2"],
    }
    in_maps = []
    for c in range(8):
        b = c // 4
        r0 = (c % 4) * cfg.OWN
        im = {"xb": x[b], "xo": np.ascontiguousarray(x[b, r0:r0 + cfg.OWN])}
        im.update(wmap)
        in_maps.append(im)
    res = run_bass_kernel_spmd(nc, in_maps, core_ids=list(range(8)))
    out = np.empty((B, S, D), dtype=np.float32)
    for c in range(8):
        b = c // 4
        r0 = (c % 4) * cfg.OWN
        out[b, r0:r0 + cfg.OWN] = res.results[c]["out"]
    return out


# version 08
# speedup vs baseline: 1.1422x; 1.1355x over previous
import contextlib
import numpy as np

import concourse.bass as bass
import concourse.mybir as mybir
from concourse.bass_utils import run_bass_kernel_spmd

F32 = mybir.dt.float32
BF16 = mybir.dt.bfloat16
AF = mybir.ActivationFunctionType
ALU = mybir.AluOpType
AX = mybir.AxisListType

ENGS = ("sync", "tensor", "vector", "scalar", "gpsimd")

D = 1024
H = 16
HF = 2048
NE = 9
NCH = D // 128


class Cfg:
    def __init__(self, SB=16384, OWN=4096, debug=False):
        self.SB = SB
        self.OWN = OWN
        self.debug = debug
        self.nT = SB // 128
        self.NQ = max(1, OWN // 1024)
        self.Q = OWN // self.NQ
        self.tbs = self.Q // 128
        self.sgs = max(1, self.Q // 512)


class Arena:

    def __init__(self, nc, es, name, nbytes):
        self.nc = nc
        self.h = es.enter_context(
            nc.sbuf_tensor(name, [128, nbytes], mybir.dt.uint8))
        self.base = nc.lookup_mloc(self.h).addr
        self.nbytes = nbytes
        self.cur = {}

    def at(self, tag, name, shape, dt):
        per = 1
        for s in shape[1:]:
            per *= s
        per *= mybir.dt.size(dt)
        off = (self.cur.get(tag, 0) + 63) & ~63
        self.cur[tag] = off + per
        assert self.cur[tag] <= self.nbytes, (name, tag, self.cur[tag])
        return self.nc.alloc_sbuf_tensor_at(name, shape, dt,
                                            offset=self.base + off)


class Prog:
    def __init__(self, nc, es):
        self.nc = nc
        self.es = es
        self.q = {e: [] for e in ENGS}
        self.sems = {}
        self.cnt = {}
        self.last_wait = {e: {} for e in ENGS}
        self.use = {}
        self.wrt = {}
        self.pend = {e: [] for e in ENGS}

    def sem(self, name):
        if name not in self.sems:
            self.sems[name] = self.es.enter_context(self.nc.semaphore(name))
            self.cnt[name] = 0
        return name

    def _raw(self, eng, fn, waits, inc, k):
        ws = []
        for s, v in waits:
            if v is None:
                continue
            assert v <= self.cnt.get(s, 0), (s, v, self.cnt.get(s, 0))
            if self.last_wait[eng].get(s, -1) < v:
                ws.append((s, v))
                self.last_wait[eng][s] = v
        mark = None
        if inc is not None:
            self.sem(inc)
            self.cnt[inc] += k
            mark = self.cnt[inc]
        self.q[eng].append((fn, tuple(ws), inc, k))
        return mark

    def op(self, eng, fn, reads=(), writes=(), extra=(), inc=None, k=1,
           wguard=True):
        waits = []
        if wguard:
            for b in writes:
                waits += list(self.use.get(b, {}).items())
                waits += list(self.wrt.get(b, {}).items())
        for b in reads:
            waits += list(self.wrt.get(b, {}).items())
        for x in extra:
            if x is not None:
                waits.append(x)
        mark = self._raw(eng, fn, waits, inc, k)
        for b in writes:
            self.pend[eng].append(("w", b))
        for b in reads:
            self.pend[eng].append(("r", b))
        if inc is not None:
            for kind, b in self.pend[eng]:
                if kind == "w":
                    self.use[b] = {}
                    self.wrt[b] = {inc: mark}
                else:
                    u = self.use.setdefault(b, {})
                    u[inc] = mark
            self.pend[eng] = []
        return mark

    def drain(self, eng, inc=None):
        return self.op(eng, lambda E: E.drain(), inc=inc)

    def replay(self, block):
        for eng in ENGS:
            deco = getattr(block, eng)

            def make(eng):
                steps = self.q[eng]
                sems = self.sems

                def run(E, steps=steps, sems=sems):
                    for fn, ws, inc, k in steps:
                        for s, v in ws:
                            E.wait_ge(sems[s], v)
                        inst = fn(E)
                        if inc is not None:
                            inst.then_inc(sems[inc], k)
                return run
            deco(make(eng))


def build_nc(cfg: Cfg):
    nc = bass.Bass("TRN2", target_bir_lowering=False)
    SB, OWN = cfg.SB, cfg.OWN

    d_xb = nc.dram_tensor("xb", [SB, D], F32, kind="ExternalInput").ap()
    d_xo = nc.dram_tensor("xo", [OWN, D], F32, kind="ExternalInput").ap()
    d_wq = nc.dram_tensor("wq", [D, D], F32, kind="ExternalInput").ap()
    d_bq = nc.dram_tensor("bq", [D], F32, kind="ExternalInput").ap()
    d_wqa = nc.dram_tensor("wqa", [D, H], F32, kind="ExternalInput").ap()
    d_bqa = nc.dram_tensor("bqa", [H], F32, kind="ExternalInput").ap()
    d_wk = nc.dram_tensor("wk", [D, D], F32, kind="ExternalInput").ap()
    d_bk = nc.dram_tensor("bk", [D], F32, kind="ExternalInput").ap()
    d_wka = nc.dram_tensor("wka", [D, H], F32, kind="ExternalInput").ap()
    d_bka = nc.dram_tensor("bka", [H], F32, kind="ExternalInput").ap()
    d_wt = nc.dram_tensor("wt", [D, D], F32, kind="ExternalInput").ap()
    d_bt = nc.dram_tensor("bt", [D], F32, kind="ExternalInput").ap()
    d_rww = nc.dram_tensor("rww", [D, 8], F32, kind="ExternalInput").ap()
    d_rwb = nc.dram_tensor("rwb", [8], F32, kind="ExternalInput").ap()
    d_sw1 = nc.dram_tensor("sw1", [D, HF], F32, kind="ExternalInput").ap()
    d_sb1 = nc.dram_tensor("sb1", [HF], F32, kind="ExternalInput").ap()
    d_sw2 = nc.dram_tensor("sw2", [HF, D], F32, kind="ExternalInput").ap()
    d_sb2 = nc.dram_tensor("sb2", [D], F32, kind="ExternalInput").ap()
    d_rw1 = nc.dram_tensor("rw1", [8, D, HF], F32, kind="ExternalInput").ap()
    d_rb1 = nc.dram_tensor("rb1", [8, HF], F32, kind="ExternalInput").ap()
    d_rw2 = nc.dram_tensor("rw2", [8, HF, D], F32, kind="ExternalInput").ap()
    d_rb2 = nc.dram_tensor("rb2", [8, D], F32, kind="ExternalInput").ap()

    d_out = nc.dram_tensor("out", [OWN, D], F32, kind="ExternalOutput").ap()

    d_xfm = nc.dram_tensor("s_xfm", [NCH, 128, SB], F32).ap()
    d_wqh = nc.dram_tensor("s_wqh", [NCH, 128, D], BF16).ap()
    d_wql = nc.dram_tensor("s_wql", [NCH, 128, D], BF16).ap()
    d_w1h = nc.dram_tensor("s_w1h", [NE, NCH, 128, HF], BF16).ap()
    d_w1l = nc.dram_tensor("s_w1l", [NE, NCH, 128, HF], BF16).ap()
    d_w2h = nc.dram_tensor("s_w2h", [NE, HF // 128, 128, D], BF16).ap()
    d_w2l = nc.dram_tensor("s_w2l", [NE, HF // 128, 128, D], BF16).ap()

    dbg = {}
    if cfg.debug:
        for nm, shp in [("dbg_pq", [128, 8]), ("dbg_pk", [128, 8]),
                        ("dbg_s1", [16, 1]), ("dbg_s2", [16, 1]),
                        ("dbg_e1", [16, D]), ("dbg_e2", [16, D]),
                        ("dbg_a", [128, 128]), ("dbg_c", [128, 128])]:
            dbg[nm] = nc.dram_tensor(nm, shp, F32, kind="ExternalOutput").ap()

    es = contextlib.ExitStack()
    with es:
        def sb(name, shape, dt=F32):
            return es.enter_context(nc.sbuf_tensor(name, shape, dt))

        ID = sb("ID", [128, 128])
        IDH = sb("IDH", [128, 128], BF16)
        PIV = sb("PIV", [128, 1])
        ONES1 = sb("ONES1", [1, 128])
        ONESH = sb("ONESH", [1, 128], BF16)
        WTH = sb("WTH", [128, NCH * D], BF16)
        WTL = sb("WTL", [128, NCH * D], BF16)
        RWH8 = sb("RWH8", [128, 64], BF16)
        RWL8 = sb("RWL8", [128, 64], BF16)
        Wqa8 = sb("Wqa8", [128, 128])
        Wka8 = sb("Wka8", [128, 128])
        WKA2 = sb("WKA2", [128, 128])
        A8 = sb("A8", [128, 128])
        C8 = sb("C8", [128, 128])
        PQ8 = sb("PQ8", [128, 8])
        PK8 = sb("PK8", [128, 8])
        BQ8 = sb("BQ8", [128, 8])
        BK8 = sb("BK8", [128, 8])
        BQAROW = sb("BQAROW", [1, H])
        BKAROW = sb("BKAROW", [1, H])
        BTROWH = sb("BTROWH", [1, D], BF16)
        BTROWL = sb("BTROWL", [1, D], BF16)
        RBROW = sb("RBROW", [1, 8])
        A0S = sb("A0S", [16, 1])
        C0S = sb("C0S", [16, 1])
        S1R = sb("S1R", [16, 1])
        S2R = sb("S2R", [16, 1])
        S1REC = sb("S1REC", [16, 1])
        S2REC = sb("S2REC", [16, 1])
        SROW = sb("SROW", [1, 16])
        TROW = sb("TROW", [1, 16])
        SML = sb("SML", [128, 32])
        CONSTS = sb("CONSTS", [128, 1])
        nc.const_aps.register(F32, 1e-5, CONSTS[:, 0:1])
        XB = [sb(f"XB{i}", [128, D]) for i in range(2)]
        XNT = [sb(f"XNT{i}", [128, D]) for i in range(2)]

        AR = Arena(nc, es, "ARENA", 149 * 1024)
        XNFM = [AR.at("s", f"XNFM{i}", [128, NCH * 512], F32) for i in range(2)]
        WSTG = [AR.at("s", f"WSTG{i}", [128, HF], F32) for i in range(2)]
        WHS = [AR.at("s", f"WHS{i}", [128, HF], BF16) for i in range(2)]
        WLS = [AR.at("s", f"WLS{i}", [128, HF], BF16) for i in range(2)]
        WSTR = [AR.at("s", f"WSTR{i}", [128, D], F32) for i in range(2)]
        WQT = AR.at("s", "WQT", [128, D], F32)
        E1SB = AR.at("s", "E1SB", [16, D], F32)
        E2SB = AR.at("s", "E2SB", [16, D], F32)
        POOL = AR.at("s", "POOL", [16, D], F32)
        S1COL = AR.at("s", "S1COL", [16, cfg.nT], F32)
        S2COL = AR.at("s", "S2COL", [16, cfg.nT], F32)
        E1T = AR.at("s", "E1T", [128, 128], F32)
        ETM = [AR.at("s", f"ETM{i}", [128, 16], F32) for i in range(2)]
        EHM = [AR.at("s", f"EHM{i}", [16, 128], F32) for i in range(2)]
        BQROW = AR.at("s", "BQROW", [1, D], F32)
        SXB = [AR.at("s", f"SXB{i}", [128, D], F32) for i in range(4)]
        SXNT = [AR.at("s", f"SXNT{i}", [128, D], F32) for i in range(4)]
        BKROW = AR.at("s", "BKROW", [1, D], F32)
        XNH = AR.at("q", "XNH", [128, NCH * 1024], BF16)
        XNL = AR.at("q", "XNL", [128, NCH * 1024], BF16)
        GACC = AR.at("q", "GACC", [128, 8 * D], F32)
        RWQA = AR.at("q", "RWQA", [128, 2 * NCH * 1024], BF16)
        RWQB = AR.at("q", "RWQB", [128, 2 * 4 * D], BF16)
        H1H = AR.at("q", "H1H", [128, 4 * 512], BF16)
        H1L = AR.at("q", "H1L", [128, 4 * 512], BF16)
        T0 = [AR.at("q", "T00", [128, 512], F32)]
        T0 = [T0[0], T0[0]]
        W8 = AR.at("q", "W8", [128, 8 * 8], F32)
        WT9 = AR.at("q", "WT9", [8, 1024], F32)
        MQHB = AR.at("q", "MQHB", [128, NCH * 128], BF16)
        MQLB = AR.at("q", "MQLB", [128, NCH * 128], BF16)
        WVH = AR.at("q", "WVH", [128, NCH * 128], BF16)
        WVL = AR.at("q", "WVL", [128, NCH * 128], BF16)
        TMQF = [AR.at("q", f"TMQF{i}", [128, 128], F32) for i in range(2)]
        RL = AR.at("q", "RL", [128, 8], F32)
        RP = AR.at("q", "RP", [128, 8], F32)
        RM = AR.at("q", "RM", [128, 2], F32)
        OUTB = AR.at("q", "OUTB", [128, D], F32)
        RB2SB2 = AR.at("q", "RB2SB2", [8, D], F32)
        SB2ROW = AR.at("q", "SB2ROW", [1, D], F32)
        BIAS1 = AR.at("q", "BIAS1", [128, 16 * NE], F32)

        PS = [es.enter_context(nc.psum_tensor(f"PS{i}", [128, 1024], F32))
              for i in range(4)]

        P = Prog(nc, es)
        for s in ("st", "pe", "dv", "sc", "gp"):
            P.sem(s)
        block = es.enter_context(nc.Block())

        def dma(eng, out, in_, reads=(), writes=(), extra=(), sem=None,
                slow=False, wguard=True):
            if sem is None:
                key = writes[0] if writes else (reads[0] if reads else "misc")
                sem = "D_" + key
            if slow:
                fn = lambda E, o=out, i=in_: E.dma_start(
                    out=o, in_=i, allow_slow_non_contiguous=True)
            else:
                fn = lambda E, o=out, i=in_: E.dma_start(out=o, in_=i)
            return P.op(eng, fn, reads=reads, writes=writes, extra=extra,
                        inc=sem, k=16, wguard=wguard)

        P.op("gpsimd", lambda E: E.iota(
            out=ID[:], pattern=[[1, 128]], base=0, channel_multiplier=0,
            allow_small_or_imprecise_dtypes=True), writes=["ID"], inc="gp")
        P.op("gpsimd", lambda E: E.iota(
            out=PIV[:], pattern=[[0, 1]], base=0, channel_multiplier=1,
            allow_small_or_imprecise_dtypes=True), writes=["PIV"], inc="gp")
        P.op("vector", lambda E: E.tensor_scalar(
            out=ID[:], in0=ID[:], scalar1=PIV[:], scalar2=None,
            op0=ALU.is_equal), reads=["ID", "PIV"], writes=["ID"], inc="dv")
        P.drain("vector")
        P.op("vector", lambda E: E.tensor_copy(out=IDH[:], in_=ID[:]),
             reads=["ID"], writes=["IDH"], inc="dv")
        P.op("vector", lambda E: E.memset(CONSTS[:, 0:1], 1e-5),
             writes=["CONSTS"])
        P.op("vector", lambda E: E.memset(ONES1[:], 1.0), writes=["ONES1"])
        P.op("vector", lambda E: E.memset(ONESH[:], 1.0), writes=["ONESH"],
             inc="dv")

        dma("sync", BQ8[:], d_bq.rearrange("(c p) -> p c", p=128),
            writes=["BQ8"], slow=True)
        dma("sync", BK8[:], d_bk.rearrange("(c p) -> p c", p=128),
            writes=["BK8"], slow=True)
        dma("sync", BQROW[:], d_bq.rearrange("(o d) -> o d", o=1), writes=["BQROW"])
        dma("sync", BKROW[:], d_bk.rearrange("(o d) -> o d", o=1), writes=["BKROW"])
        dma("sync", BQAROW[:], d_bqa.rearrange("(o d) -> o d", o=1), writes=["BQAROW"])
        dma("sync", BKAROW[:], d_bka.rearrange("(o d) -> o d", o=1), writes=["BKAROW"])
        dma("sync", RBROW[:], d_rwb.rearrange("(o d) -> o d", o=1), writes=["RBROW"])
        for c in range(NCH):
            dma("sync", Wqa8[:, 16 * c:16 * (c + 1)],
                d_wqa[128 * c:128 * (c + 1), :], writes=["Wqa8"])
            dma("sync", Wka8[:, 16 * c:16 * (c + 1)],
                d_wka[128 * c:128 * (c + 1), :], writes=["Wka8"])
        def barrier():
            mpe = P.drain("tensor", inc="pe")
            mdv = P.drain("vector", inc="dv")
            msc = P.drain("scalar", inc="sc")
            mgp = P.drain("gpsimd", inc="gp")
            waits = [("pe", mpe), ("dv", mdv), ("sc", msc), ("gp", mgp)]
            waits += [(nm, c) for nm, c in P.cnt.items()
                      if (nm.startswith("D_") or nm == "st") and c > 0]
            for eng in ENGS:
                P._raw(eng, lambda E: E.nop(), waits=waits, inc=None, k=1)

        for k in range(NCH):
            dma("sync", WSTR[k % 2][:], d_wq[128 * k:128 * (k + 1), :],
                writes=[f"WSTR{k % 2}"])
            for j in range(NCH):
                mk = P.op("tensor", lambda E, k=k, j=j: E.transpose(
                    out=PS[2][:, 128 * j:128 * (j + 1)],
                    in_=WSTR[k % 2][:, 128 * j:128 * (j + 1)], identity=ID[:]),
                    reads=[f"WSTR{k % 2}", "ID"],
                    writes=["PS2A", "PS2B"],
                    inc="pe" if j == NCH - 1 else None)
            P.op("vector", lambda E: E.tensor_copy(out=WQT[:], in_=PS[2][:]),
                 reads=["PS2A", "PS2B"], writes=["WQT"], inc="dv")
            for j in range(NCH):
                mk = P.op("tensor", lambda E, k=k, j=j: E.matmul(
                    out=PS[1][0:128, 0:16], lhsT=WQT[:, 128 * j:128 * (j + 1)],
                    rhs=Wqa8[:, 16 * j:16 * (j + 1)],
                    start=(j == 0), stop=(j == NCH - 1)),
                    reads=["WQT", "Wqa8"], writes=["PS1A"],
                    inc="pe" if j == NCH - 1 else None)
            P.op("vector", lambda E, k=k: E.tensor_copy(
                out=A8[:, 16 * k:16 * (k + 1)], in_=PS[1][0:128, 0:16]),
                reads=["PS1A"], writes=["A8"], inc="dv")
            b2 = k % 2
            P.op("scalar", lambda E, k=k, b2=b2: E.copy(
                out=WHS[b2][:, 0:D], in_=WSTR[b2][:]),
                reads=[f"WSTR{b2}"], writes=[f"WHS{b2}"], inc="sc")
            P.op("vector", lambda E, k=k, b2=b2: E.tensor_tensor(
                out=WLS[b2][:, 0:D], in0=WSTR[b2][:],
                in1=WHS[b2][:, 0:D], op=ALU.subtract),
                reads=[f"WSTR{b2}", f"WHS{b2}"], writes=[f"WLS{b2}"], inc="dv")
            dma("gpsimd", d_wqh[k], WHS[b2][:, 0:D], reads=[f"WHS{b2}"],
                writes=["d_wqh"], sem=f"D_spl{b2}")
            dma("gpsimd", d_wql[k], WLS[b2][:, 0:D],
                reads=[f"WLS{b2}", f"WHS{b2}"],
                writes=["d_wql", "d_wqh"], sem=f"D_spl{b2}")
        for j in range(NCH):
            P.op("tensor", lambda E, j=j: E.matmul(
                out=PS[1][0:1, 512:528], lhsT=BQ8[:, j:j + 1],
                rhs=Wqa8[:, 16 * j:16 * (j + 1)],
                start=(j == 0), stop=(j == NCH - 1)),
                reads=["BQ8", "Wqa8"], writes=["PS1B"],
                inc="pe" if j == NCH - 1 else None)
        P.op("vector", lambda E: E.tensor_tensor(
            out=TROW[:], in0=PS[1][0:1, 512:528], in1=BQAROW[:], op=ALU.add),
            reads=["PS1B", "BQAROW"], writes=["TROW"], inc="dv")
        P.op("tensor", lambda E: E.transpose(
            out=PS[1][0:16, 528:529], in_=TROW[:], identity=ID[0:1, 0:1]),
            reads=["TROW", "ID"], writes=["PS1B"], inc="pe")
        P.op("vector", lambda E: E.tensor_scalar(
            out=A0S[:], in0=PS[1][0:16, 528:529], scalar1=0.125, scalar2=None,
            op0=ALU.mult), reads=["PS1B"], writes=["A0S"], inc="dv")

        split_jobs = []
        for k in range(NCH):
            split_jobs.append(("res", d_wt[128 * k:128 * (k + 1), :], D,
                               WTH, WTL, D * k, "WTH", "WTL"))
        split_jobs.append(("resm", [d_rww[128 * c:128 * (c + 1), :]
                                    for c in range(NCH)],
                           64, RWH8, RWL8, 0, "RWH8", "RWL8"))
        for e in range(NE):
            s1 = d_sw1 if e == 8 else d_rw1[e]
            for c in range(NCH):
                split_jobs.append(("dram", s1[128 * c:128 * (c + 1), :], HF,
                                   d_w1h[e, c], d_w1l[e, c], None, "w1h", "w1l"))
            s2 = d_sw2 if e == 8 else d_rw2[e]
            for c in range(HF // 128):
                split_jobs.append(("dram", s2[128 * c:128 * (c + 1), :], D,
                                   d_w2h[e, c], d_w2l[e, c], None, "w2h", "w2l"))

        def emit_split(u):
            kind, src, width, dh, dl, off, kh, kl = split_jobs[u]
            b = u % 2
            if kind == "resm":
                w0 = width // len(src)
                for ci, sap in enumerate(src):
                    dma("sync", WSTG[b][:, w0 * ci:w0 * (ci + 1)], sap,
                        writes=[f"WSTG{b}"])
                kind = "res"
            else:
                dma("sync", WSTG[b][:, 0:width], src, writes=[f"WSTG{b}"])
            if kind == "res":
                P.op("scalar", lambda E: E.copy(
                    out=dh[:, off:off + width], in_=WSTG[b][:, 0:width]),
                    reads=[f"WSTG{b}"], writes=[kh], inc="sc")
                P.op("vector", lambda E: E.tensor_tensor(
                    out=dl[:, off:off + width], in0=WSTG[b][:, 0:width],
                    in1=dh[:, off:off + width], op=ALU.subtract),
                    reads=[f"WSTG{b}", kh], writes=[kl], inc="dv")
            else:
                P.op("scalar", lambda E: E.copy(
                    out=WHS[b][:, 0:width], in_=WSTG[b][:, 0:width]),
                    reads=[f"WSTG{b}"], writes=[f"WHS{b}"], inc="sc")
                subeng, subsem = (("vector", "dv") if (u // 2) % 2 == 0
                                  else ("gpsimd", "gp"))
                P.op(subeng, lambda E: E.tensor_tensor(
                    out=WLS[b][:, 0:width], in0=WSTG[b][:, 0:width],
                    in1=WHS[b][:, 0:width], op=ALU.subtract),
                    reads=[f"WSTG{b}", f"WHS{b}"], writes=[f"WLS{b}"],
                    inc=subsem)
                dma("gpsimd", dh, WHS[b][:, 0:width], reads=[f"WHS{b}"],
                    writes=[kh], sem=f"D_spl{b}")
                dma("gpsimd", dl, WLS[b][:, 0:width],
                    reads=[f"WLS{b}", f"WHS{b}"],
                    writes=[kl, kh], sem=f"D_spl{b}")

        def emit_ln(src_ap, xkey, xbuf, dkey, dst, par=0):
            SL = SML[:, 8 * par:8 * par + 8]
            skey = f"SML{par}"
            dma("sync", xbuf[:], src_ap, writes=[xkey])
            P.op("vector", lambda E: E.tensor_reduce(
                out=SL[:, 0:1], in_=xbuf[:], axis=AX.X, op=ALU.add),
                reads=[xkey], writes=[skey], inc="dv")
            P.op("scalar", lambda E: E.activation(
                out=dst[:], in_=xbuf[:], func=AF.Square, bias=0.0, scale=1.0,
                accum_out=SL[:, 4:5]),
                reads=[xkey], writes=[dkey, skey], inc="sc")
            P.op("vector", lambda E: E.tensor_scalar(
                out=SL[:, 1:2], in0=SL[:, 0:1], scalar1=1.0 / D,
                scalar2=None, op0=ALU.mult), reads=[skey], writes=[skey])
            P.drain("vector")
            P.op("vector", lambda E: E.tensor_tensor(
                out=SL[:, 2:3], in0=SL[:, 1:2], in1=SL[:, 1:2],
                op=ALU.mult))
            P.op("vector", lambda E: E.tensor_scalar(
                out=SL[:, 3:4], in0=SL[:, 4:5], scalar1=1.0 / D,
                scalar2=None, op0=ALU.mult))
            P.drain("vector")
            P.op("vector", lambda E: E.tensor_tensor(
                out=SL[:, 3:4], in0=SL[:, 3:4], in1=SL[:, 2:3],
                op=ALU.subtract), writes=[skey], inc="dv")
            P.op("scalar", lambda E: E.activation(
                out=SL[:, 5:6], in_=SL[:, 3:4], func=AF.Sqrt, bias=1e-5,
                scale=1.0), reads=[skey], writes=[skey], inc="sc")
            P.op("vector", lambda E: E.reciprocal(
                out=SL[:, 6:7], in_=SL[:, 5:6]),
                reads=[skey], writes=[skey])
            P.drain("vector")
            m = P.op("vector", lambda E: E.tensor_scalar(
                out=dst[:], in0=xbuf[:], scalar1=SL[:, 1:2],
                scalar2=SL[:, 6:7], op0=ALU.subtract, op1=ALU.mult),
                reads=[xkey, skey], writes=[dkey], inc="dv")
            return m

        def emit_spine(pass1):
            def emit_tail(t):
                tb4 = t % 4
                tb = t % 2
                qo = 512 * (t % 2)
                rkey = "PS1A" if t % 2 == 0 else "PS1B"
                P.op("tensor", lambda E, tb=tb, qo=qo: E.transpose(
                    out=PS[1][0:128, qo + 128:qo + 144], in_=EHM[tb][:],
                    identity=ID[0:16, 0:16]),
                    reads=[f"EHM{tb}", "ID"], writes=[rkey], inc="pe")
                P.op("vector", lambda E, tb=tb, qo=qo: E.tensor_copy(
                    out=ETM[tb][:], in_=PS[1][0:128, qo + 128:qo + 144]),
                    reads=[rkey], writes=[f"ETM{tb}"], inc="dv")
                for h in range(2):
                    P.op("tensor", lambda E, tb4=tb4, tb=tb, h=h, t=t: E.matmul(
                        out=PS[0][0:16, 512 * h:512 * (h + 1)], lhsT=ETM[tb][:],
                        rhs=SXNT[tb4][:, 512 * h:512 * (h + 1)],
                        start=(t == 0), stop=(t == cfg.nT - 1)),
                        reads=[f"ETM{tb}", f"SXNT{tb4}"],
                        writes=["PS0A" if h == 0 else "PS0B"],
                        inc="pe" if h == 1 else None)

            AB = A8 if pass1 else C8
            BSC = A0S if pass1 else C0S
            SCOL = S1COL if pass1 else S2COL
            for t in range(cfg.nT):
                g, tb4, tb, gb = t // 4, t % 4, t % 2, (t // 4) % 2
                c0 = 128 * (t % 4)
                qo = 512 * (t % 2)
                if pass1:
                    emit_ln(d_xb[128 * t:128 * (t + 1), :], f"SXB{tb4}",
                            SXB[tb4], f"SXNT{tb4}", SXNT[tb4], par=tb4)
                    pst = PS[2 + (t % 2)]
                    pkeys = [f"PS{2 + (t % 2)}A", f"PS{2 + (t % 2)}B"]
                    for j in range(NCH):
                        P.op("tensor", lambda E, j=j, pst=pst, tb4=tb4:
                             E.transpose(
                                 out=pst[:, 128 * j:128 * (j + 1)],
                                 in_=SXNT[tb4][:, 128 * j:128 * (j + 1)],
                                 identity=ID[:]),
                             reads=[f"SXNT{tb4}", "ID"], writes=pkeys,
                             inc="pe" if j == NCH - 1 else None)
                    for j in range(NCH):
                        P.op("scalar", lambda E, j=j, pst=pst, gb=gb, c0=c0:
                             E.copy(out=XNFM[gb][:, 512 * j + c0:
                                                 512 * j + c0 + 128],
                                    in_=pst[:, 128 * j:128 * (j + 1)]),
                             reads=pkeys, writes=[f"XNFM{gb}"],
                             inc="sc" if j == NCH - 1 else None)
                else:
                    if t % 4 == 0:
                        for j in range(NCH):
                            dma("sync", XNFM[gb][:, 512 * j:512 * (j + 1)],
                                d_xfm[j, :, 512 * g:512 * (g + 1)],
                                reads=["xfm"], writes=[f"XNFM{gb}"],
                                wguard=(j == 0))
                    pst = PS[2 + (t % 2)]
                    pkeys = [f"PS{2 + (t % 2)}A", f"PS{2 + (t % 2)}B"]
                    for j in range(NCH):
                        P.op("tensor", lambda E, j=j, pst=pst, gb=gb, c0=c0:
                             E.transpose(
                                 out=pst[:, 128 * j:128 * (j + 1)],
                                 in_=XNFM[gb][:, 512 * j + c0:
                                              512 * j + c0 + 128],
                                 identity=ID[:]),
                             reads=[f"XNFM{gb}", "ID"], writes=pkeys,
                             inc="pe" if j == NCH - 1 else None)
                    for j in range(NCH):
                        P.op("scalar", lambda E, j=j, pst=pst, tb4=tb4:
                             E.copy(
                                 out=SXNT[tb4][:, 128 * j:128 * (j + 1)],
                                 in_=pst[:, 128 * j:128 * (j + 1)]),
                             reads=pkeys, writes=[f"SXNT{tb4}"],
                             inc="sc" if j == NCH - 1 else None)
                rkey = "PS1A" if t % 2 == 0 else "PS1B"
                for j in range(NCH):
                    P.op("tensor", lambda E, j=j, qo=qo, gb=gb, c0=c0, AB=AB:
                         E.matmul(
                             out=PS[1][0:16, qo:qo + 128],
                             lhsT=AB[:, 16 * j:16 * (j + 1)],
                             rhs=XNFM[gb][:, 512 * j + c0:512 * j + c0 + 128],
                             start=(j == 0), stop=(j == NCH - 1)),
                         reads=[f"XNFM{gb}", "A8" if pass1 else "C8"],
                         writes=[rkey],
                         inc="pe" if j == NCH - 1 else None)
                P.op("scalar", lambda E, t=t, tb=tb, qo=qo, BSC=BSC, SCOL=SCOL:
                     E.activation(out=EHM[tb][:], in_=PS[1][0:16, qo:qo + 128],
                                  func=AF.Exp, bias=BSC[:], scale=0.125,
                                  accum_out=SCOL[:, t:t + 1]),
                     reads=[rkey, "A0S" if pass1 else "C0S"],
                     writes=[f"EHM{tb}", "S1COL" if pass1 else "S2COL"],
                     inc="sc")
                if t >= 1:
                    emit_tail(t - 1)
                if pass1:
                    if t % 4 == 3:
                        for j in range(NCH):
                            dma("gpsimd", d_xfm[j, :, 512 * g:512 * (g + 1)],
                                XNFM[gb][:, 512 * j:512 * (j + 1)],
                                reads=[f"XNFM{gb}"], writes=["xfm"],
                                sem=f"D_xfm{gb}", wguard=(j == 0))
                base = 0 if pass1 else (len(split_jobs) * 55) // 100
                top = (len(split_jobs) * 55) // 100 if pass1 else len(split_jobs)
                quota = base + ((t + 1) * (top - base)) // cfg.nT
                while emit_spine.spl_u < quota:
                    emit_split(emit_spine.spl_u)
                    emit_spine.spl_u += 1
            emit_tail(cfg.nT - 1)
            SR = S1R if pass1 else S2R
            SREC = S1REC if pass1 else S2REC
            ESB = E1SB if pass1 else E2SB
            P.op("vector", lambda E, SR=SR, SCOL=SCOL: E.tensor_reduce(
                out=SR[:], in_=SCOL[:], axis=AX.X, op=ALU.add),
                reads=["S1COL" if pass1 else "S2COL"],
                writes=["S1R" if pass1 else "S2R"], inc="dv")
            P.drain("vector")
            P.op("vector", lambda E, SR=SR, SREC=SREC: E.reciprocal(
                out=SREC[:], in_=SR[:]),
                reads=["S1R" if pass1 else "S2R"],
                writes=["S1REC" if pass1 else "S2REC"], inc="dv")
            P.op("vector", lambda E, ESB=ESB: E.tensor_copy(
                out=ESB[:], in_=PS[0][0:16, :]),
                reads=["PS0A", "PS0B"], writes=["E1SB" if pass1 else "E2SB"],
                inc="dv")

        emit_spine.spl_u = 0
        emit_spine(True)

        def emit_pool(tag, ESBk, ESB, SR, SRECk, SREC, brow_key, brow, PX_key,
                      PX, mul_pq):
            skey = "S1R" if tag == "q" else "S2R"
            P.op("tensor", lambda E: E.transpose(
                out=PS[1][0:1, 0:16], in_=SR[:], identity=ID[0:16, 0:16]),
                reads=[skey, "ID"], writes=["PS1A"], inc="pe")
            P.op("vector", lambda E: E.tensor_copy(
                out=SROW[:], in_=PS[1][0:1, 0:16]),
                reads=["PS1A"], writes=["SROW"], inc="dv")
            for k in range(NCH):
                P.op("tensor", lambda E, k=k: E.transpose(
                    out=PS[1][0:128, 16 * k + 32:16 * k + 48],
                    in_=ESB[:, 128 * k:128 * (k + 1)],
                    identity=ID[0:16, 0:16]),
                    reads=[ESBk, "ID"], writes=["PS1A"], inc="pe")
                P.op("vector", lambda E, k=k: E.tensor_copy(
                    out=E1T[:, 16 * k:16 * (k + 1)],
                    in_=PS[1][0:128, 16 * k + 32:16 * k + 48]),
                    reads=["PS1A"], writes=["E1T"], inc="dv")
            wsrc = d_wq if tag == "q" else d_wk
            for k in range(NCH):
                dma("sync", WSTR[k % 2][:], wsrc[128 * k:128 * (k + 1), :],
                    writes=[f"WSTR{k % 2}"])
                for h in range(2):
                    P.op("tensor", lambda E, k=k, h=h: E.matmul(
                        out=PS[0][0:16, 512 * h:512 * (h + 1)],
                        lhsT=E1T[:, 16 * k:16 * (k + 1)],
                        rhs=WSTR[k % 2][:, 512 * h:512 * (h + 1)],
                        start=(k == 0), stop=False),
                        reads=["E1T", f"WSTR{k % 2}"],
                        writes=["PS0A" if h == 0 else "PS0B"],
                        inc="pe" if h == 1 else None)
            for h in range(2):
                P.op("tensor", lambda E, h=h: E.matmul(
                    out=PS[0][0:16, 512 * h:512 * (h + 1)], lhsT=SROW[:],
                    rhs=brow[0:1, 512 * h:512 * (h + 1)],
                    start=False, stop=True),
                    reads=["SROW", brow_key],
                    writes=["PS0A" if h == 0 else "PS0B"],
                    inc="pe" if h == 1 else None)
            P.op("vector", lambda E: E.tensor_scalar(
                out=POOL[:], in0=PS[0][0:16, :], scalar1=SREC[:],
                scalar2=None, op0=ALU.mult),
                reads=["PS0A", "PS0B", SRECk], writes=["POOL"], inc="dv")
            for k in range(NCH):
                P.op("tensor", lambda E, k=k: E.transpose(
                    out=PS[1][0:128, 16 * k + 64:16 * k + 80],
                    in_=POOL[:, 128 * k:128 * (k + 1)],
                    identity=ID[0:16, 0:16]),
                    reads=["POOL", "ID"], writes=["PS1B"], inc="pe")
                P.op("vector", lambda E, k=k: E.tensor_copy(
                    out=PX[0:64, k:k + 1],
                    in_=PS[1][0:64, 16 * k + 64 + 2 * k:16 * k + 65 + 2 * k]),
                    reads=["PS1B"], writes=[PX_key])
                P.op("vector", lambda E, k=k: E.tensor_copy(
                    out=PX[64:128, k:k + 1],
                    in_=PS[1][64:128, 16 * k + 65 + 2 * k:16 * k + 66 + 2 * k]),
                    reads=["PS1B"], writes=[PX_key], inc="dv")
            if mul_pq:
                P.op("vector", lambda E: E.tensor_tensor(
                    out=PX[:], in0=PX[:], in1=PQ8[:], op=ALU.mult),
                    reads=[PX_key, "PQ8"], writes=[PX_key], inc="dv")

        emit_pool("q", "E1SB", E1SB, S1R, "S1REC", S1REC, "BQROW", BQROW,
                  "PQ8", PQ8, False)

        for k in range(NCH):
            P.op("vector", lambda E, k=k: E.tensor_scalar(
                out=WKA2[:, 16 * k:16 * (k + 1)],
                in0=Wka8[:, 16 * k:16 * (k + 1)],
                scalar1=PQ8[:, k:k + 1], scalar2=None, op0=ALU.mult),
                reads=["Wka8", "PQ8"], writes=["WKA2"],
                inc="dv" if k == NCH - 1 else None)
        for k in range(NCH):
            dma("sync", WSTR[k % 2][:], d_wk[128 * k:128 * (k + 1), :],
                writes=[f"WSTR{k % 2}"])
            for j in range(NCH):
                P.op("tensor", lambda E, k=k, j=j: E.transpose(
                    out=PS[2][:, 128 * j:128 * (j + 1)],
                    in_=WSTR[k % 2][:, 128 * j:128 * (j + 1)], identity=ID[:]),
                    reads=[f"WSTR{k % 2}", "ID"], writes=["PS2A", "PS2B"],
                    inc="pe" if j == NCH - 1 else None)
            P.op("vector", lambda E: E.tensor_copy(out=WQT[:], in_=PS[2][:]),
                 reads=["PS2A", "PS2B"], writes=["WQT"], inc="dv")
            for j in range(NCH):
                P.op("tensor", lambda E, k=k, j=j: E.matmul(
                    out=PS[1][0:128, 0:16], lhsT=WQT[:, 128 * j:128 * (j + 1)],
                    rhs=WKA2[:, 16 * j:16 * (j + 1)],
                    start=(j == 0), stop=(j == NCH - 1)),
                    reads=["WQT", "WKA2"], writes=["PS1A"],
                    inc="pe" if j == NCH - 1 else None)
            P.op("vector", lambda E, k=k: E.tensor_copy(
                out=C8[:, 16 * k:16 * (k + 1)], in_=PS[1][0:128, 0:16]),
                reads=["PS1A"], writes=["C8"], inc="dv")
        for j in range(NCH):
            P.op("tensor", lambda E, j=j: E.matmul(
                out=PS[1][0:1, 512:528], lhsT=BK8[:, j:j + 1],
                rhs=WKA2[:, 16 * j:16 * (j + 1)],
                start=(j == 0), stop=(j == NCH - 1)),
                reads=["BK8", "WKA2"], writes=["PS1B"],
                inc="pe" if j == NCH - 1 else None)
        P.op("vector", lambda E: E.tensor_tensor(
            out=TROW[:], in0=PS[1][0:1, 512:528], in1=BKAROW[:], op=ALU.add),
            reads=["PS1B", "BKAROW"], writes=["TROW"], inc="dv")
        P.op("tensor", lambda E: E.transpose(
            out=PS[1][0:16, 528:529], in_=TROW[:], identity=ID[0:1, 0:1]),
            reads=["TROW", "ID"], writes=["PS1B"], inc="pe")
        P.op("vector", lambda E: E.tensor_scalar(
            out=C0S[:], in0=PS[1][0:16, 528:529], scalar1=0.125, scalar2=None,
            op0=ALU.mult), reads=["PS1B"], writes=["C0S"], inc="dv")

        emit_spine(False)
        emit_pool("k", "E2SB", E2SB, S2R, "S2REC", S2REC, "BKROW", BKROW,
                  "PK8", PK8, True)

        if cfg.debug:
            dma("gpsimd", dbg["dbg_pq"], PQ8[:], reads=["PQ8"], sem="st")
            dma("gpsimd", dbg["dbg_pk"], PK8[:], reads=["PK8"], sem="st")
            dma("gpsimd", dbg["dbg_s1"], S1R[:], reads=["S1R"], sem="st")
            dma("gpsimd", dbg["dbg_s2"], S2R[:], reads=["S2R"], sem="st")
            dma("gpsimd", dbg["dbg_e1"], E1SB[:], reads=["E1SB"], sem="st")
            dma("gpsimd", dbg["dbg_e2"], E2SB[:], reads=["E2SB"], sem="st")
            dma("gpsimd", dbg["dbg_a"], A8[:], reads=["A8"], sem="st")
            dma("gpsimd", dbg["dbg_c"], C8[:], reads=["C8"], sem="st")

        barrier()
        dma("sync", RB2SB2[0:8, :], d_rb2, writes=["RB2SB2"])
        dma("sync", SB2ROW[:], d_sb2.rearrange("(o d) -> o d", o=1),
            writes=["SB2ROW"])
        for e in range(8):
            dma("sync", BIAS1[:, 16 * e:16 * e + 16],
                d_rb1[e].rearrange("(c p) -> p c", p=128),
                writes=["BIAS1"], slow=True)
        dma("sync", BIAS1[:, 16 * 8:16 * 9],
            d_sb1.rearrange("(c p) -> p c", p=128), writes=["BIAS1"], slow=True)
        dma("sync", OUTB[0:1, :], d_bt.rearrange("(o d) -> o d", o=1), writes=["OUTB"])
        P.op("vector", lambda E: E.tensor_copy(out=BTROWH[:], in_=OUTB[0:1, :]),
             reads=["OUTB"], writes=["BTROWH"], inc="dv")
        P.drain("vector")
        P.op("vector", lambda E: E.tensor_tensor(
            out=BTROWL[:], in0=OUTB[0:1, :], in1=BTROWH[:], op=ALU.subtract),
            reads=["OUTB", "BTROWH"], writes=["BTROWL"], inc="dv")

        HB = 4
        NHQ = HF // 512
        e_order = [8] + list(range(8))

        for q in range(cfg.NQ):
            q0 = q * cfg.Q

            for tb in range(cfg.tbs):
                t0c = 128 * tb
                b = tb % 2
                emit_ln(d_xo[q0 + t0c:q0 + t0c + 128, :], f"XB{b}", XB[b],
                        f"XNT{b}", XNT[b], par=b)
                pst = PS[2]
                pkeys = ["PS2A", "PS2B"]
                for j in range(NCH):
                    P.op("tensor", lambda E, j=j, pst=pst, b=b: E.transpose(
                        out=pst[:, 128 * j:128 * (j + 1)],
                        in_=XNT[b][:, 128 * j:128 * (j + 1)], identity=ID[:]),
                        reads=[f"XNT{b}", "ID"], writes=pkeys,
                        inc="pe" if j == NCH - 1 else None)
                for j in range(NCH):
                    P.op("scalar", lambda E, j=j, pst=pst, t0c=t0c: E.copy(
                        out=XNH[:, 1024 * j + t0c:1024 * j + t0c + 128],
                        in_=pst[:, 128 * j:128 * (j + 1)]),
                        reads=pkeys, writes=["XNH"],
                        inc="sc" if j == NCH - 1 else None)
                for j in range(NCH):
                    P.op("vector", lambda E, j=j, pst=pst, t0c=t0c:
                         E.tensor_tensor(
                             out=XNL[:, 1024 * j + t0c:1024 * j + t0c + 128],
                             in0=pst[:, 128 * j:128 * (j + 1)],
                             in1=XNH[:, 1024 * j + t0c:1024 * j + t0c + 128],
                             op=ALU.subtract),
                         reads=pkeys + ["XNH"], writes=["XNL"],
                         inc="dv" if j == NCH - 1 else None)
                P.op("tensor", lambda E: E.matmul(
                    out=PS[3][0:128, 0:8], lhsT=ONES1[:], rhs=RBROW[:],
                    start=True, stop=False),
                    reads=["ONES1", "RBROW"], writes=["PS3A"])
                for j in range(NCH):
                    for pi, (xa, wa, xk, wk_) in enumerate((
                            (XNH, RWH8, "XNH", "RWH8"),
                            (XNH, RWL8, "XNH", "RWL8"),
                            (XNL, RWH8, "XNL", "RWH8"))):
                        P.op("tensor", lambda E, j=j, xa=xa, wa=wa, t0c=t0c,
                             sp=(j == NCH - 1 and pi == 2):
                             E.matmul(
                                 out=PS[3][0:128, 0:8],
                                 lhsT=xa[:, 1024 * j + t0c:1024 * j + t0c + 128],
                                 rhs=wa[:, 8 * j:8 * (j + 1)],
                                 start=False, stop=sp),
                             reads=[xk, wk_], writes=["PS3A"],
                             inc="pe" if (j == NCH - 1 and pi == 2) else None)
                P.op("vector", lambda E: E.tensor_copy(
                    out=RL[:], in_=PS[3][0:128, 0:8]),
                    reads=["PS3A"], writes=["RL"], inc="dv")
                P.drain("vector")
                P.op("vector", lambda E: E.reduce_max(
                    out=RM[:, 0:1], in_=RL[:], axis=AX.X))
                P.drain("vector")
                P.op("vector", lambda E: E.tensor_scalar(
                    out=RP[:], in0=RL[:], scalar1=RM[:, 0:1], scalar2=None,
                    op0=ALU.is_equal))
                P.drain("vector")
                P.op("vector", lambda E: E.scalar_tensor_tensor(
                    out=RP[:], in0=RP[:], scalar=-1e30, in1=RL[:],
                    op0=ALU.mult, op1=ALU.add))
                P.drain("vector")
                P.op("vector", lambda E: E.reduce_max(
                    out=RM[:, 1:2], in_=RP[:], axis=AX.X, negate=True))
                P.drain("vector")
                P.op("vector", lambda E: E.tensor_scalar(
                    out=RP[:], in0=RL[:], scalar1=RM[:, 1:2], scalar2=0.0,
                    op0=ALU.add, op1=ALU.is_ge))
                P.drain("vector")
                P.op("vector", lambda E: E.reduce_max(
                    out=RM[:, 0:1], in_=RL[:], axis=AX.X, negate=True),
                    writes=["RM"], reads=["RL"], inc="dv")
                P.op("scalar", lambda E: E.activation(
                    out=RL[:], in_=RL[:], func=AF.Exp, bias=RM[:, 0:1],
                    scale=1.0, accum_out=RM[:, 1:2]),
                    reads=["RL", "RM"], writes=["RL", "RM"], inc="sc")
                P.op("vector", lambda E: E.tensor_tensor(
                    out=RP[:], in0=RP[:], in1=RL[:], op=ALU.mult),
                    reads=["RP", "RL"], writes=["RP"])
                P.drain("vector")
                P.op("vector", lambda E: E.reciprocal(
                    out=RM[:, 0:1], in_=RM[:, 1:2]), reads=["RM"],
                    writes=["RM"])
                P.drain("vector")
                P.op("vector", lambda E, tb=tb: E.tensor_scalar(
                    out=W8[:, 8 * tb:8 * (tb + 1)], in0=RP[:],
                    scalar1=RM[:, 0:1], scalar2=None, op0=ALU.mult),
                    reads=["RP", "RM"], writes=["W8"], inc="dv")
                P.op("tensor", lambda E, tb=tb: E.transpose(
                    out=PS[3][0:8, 512:640], in_=W8[:, 8 * tb:8 * (tb + 1)],
                    identity=ID[:]),
                    reads=["W8", "ID"], writes=["PS3B"], inc="pe")
                P.op("vector", lambda E, tb=tb: E.tensor_copy(
                    out=WT9[:, 128 * tb:128 * (tb + 1)],
                    in_=PS[3][0:8, 512:640]),
                    reads=["PS3B"], writes=["WT9"], inc="dv")

            for ei, e in enumerate(e_order):
                for hq in range(NHQ):
                    if hq % 2 == 0:
                        for c in range(NCH):
                            dma("sync", RWQA[:, 1024 * c:1024 * (c + 1)],
                                d_w1h[e, c, :, 1024 * (hq // 2):
                                      1024 * (hq // 2) + 1024],
                                reads=["w1h"], writes=["RWQA"],
                                wguard=(c == 0))
                            dma("sync",
                                RWQA[:, 8192 + 1024 * c:8192 + 1024 * (c + 1)],
                                d_w1l[e, c, :, 1024 * (hq // 2):
                                      1024 * (hq // 2) + 1024],
                                reads=["w1l"], writes=["RWQA"],
                                wguard=False)
                    for c in range(4):
                        dma("sync", RWQB[:, 1024 * c:1024 * (c + 1)],
                            d_w2h[e, 4 * hq + c], reads=["w2h"],
                            writes=["RWQB"], wguard=(c == 0))
                        dma("sync", RWQB[:, 4096 + 1024 * c:4096 + 1024 * (c + 1)],
                            d_w2l[e, 4 * hq + c], reads=["w2l"],
                            writes=["RWQB"], wguard=False)
                    for sg in range(cfg.sgs):
                        for hb in range(HB):
                            hbp = hb % 2
                            reg = PS[0][:, 512 * hbp:512 * hbp + 512]
                            rkey = "PS0A" if hbp == 0 else "PS0B"
                            for j in range(NCH):
                                xsl = slice(1024 * j + 512 * sg,
                                            1024 * j + 512 * sg + 512)
                                wsl = slice(1024 * j + 512 * (hq % 2) + 128 * hb,
                                            1024 * j + 512 * (hq % 2) + 128 * hb
                                            + 128)
                                for pi, (woff, xa, xk) in enumerate((
                                        (0, XNH, "XNH"),
                                        (0, XNL, "XNL"),
                                        (8192, XNH, "XNH"))):
                                    P.op("tensor",
                                         lambda E, woff=woff, xa=xa, wsl=wsl,
                                         xsl=xsl, reg=reg,
                                         st=(j == 0 and pi == 0),
                                         sp=(j == NCH - 1 and pi == 2):
                                         E.matmul(
                                             out=reg,
                                             lhsT=RWQA[:, woff + wsl.start:
                                                       woff + wsl.stop],
                                             rhs=xa[:, xsl],
                                             start=st, stop=sp),
                                         reads=["RWQA", xk], writes=[rkey],
                                         inc="pe" if (j == NCH - 1 and pi == 2)
                                         else None)
                            bcol = 16 * e + 4 * hq + hb
                            P.op("scalar", lambda E, reg=reg, hbp=hbp,
                                 bcol=bcol: E.activation(
                                     out=T0[hbp][:], in_=reg, func=AF.Gelu,
                                     bias=BIAS1[:, bcol:bcol + 1], scale=1.0),
                                 reads=[rkey, "BIAS1"], writes=["T00"],
                                 inc="sc")
                            hsl = slice(512 * hb, 512 * hb + 512)
                            P.op("vector", lambda E, hbp=hbp, hsl=hsl:
                                 E.tensor_copy(out=H1H[:, hsl],
                                               in_=T0[hbp][:]),
                                 reads=["T00"], writes=[f"H1H{hb}"],
                                 inc="dv")
                            P.drain("vector")
                            P.op("vector", lambda E, hbp=hbp, hsl=hsl:
                                 E.tensor_tensor(out=H1L[:, hsl],
                                                 in0=T0[hbp][:],
                                                 in1=H1H[:, hsl],
                                                 op=ALU.subtract),
                                 reads=["T00", f"H1H{hb}"],
                                 writes=[f"H1L{hb}"], inc="dv")
                        first = (ei == 0 and hq == 0)
                        for tb4 in range(4):
                            gtb = 4 * sg + tb4
                            pst = PS[1 + (gtb % 2)]
                            pk = [f"PS{1 + (gtb % 2)}A", f"PS{1 + (gtb % 2)}B"]
                            if first:
                                for h in range(2):
                                    P.op("tensor", lambda E, h=h, gtb=gtb,
                                         pst=pst: E.matmul(
                                             out=pst[:, 512 * h:512 * (h + 1)],
                                             lhsT=WT9[:, 128 * gtb:
                                                      128 * (gtb + 1)],
                                             rhs=RB2SB2[:, 512 * h:
                                                        512 * (h + 1)],
                                             start=True, stop=False),
                                         reads=["WT9", "RB2SB2"], writes=pk)
                                    P.op("tensor", lambda E, h=h, pst=pst:
                                         E.matmul(
                                             out=pst[:, 512 * h:512 * (h + 1)],
                                             lhsT=ONES1[:, 0:128],
                                             rhs=SB2ROW[0:1, 512 * h:
                                                        512 * (h + 1)],
                                             start=False, stop=False),
                                         reads=["ONES1", "SB2ROW"], writes=pk)
                            for hc in range(HB):
                                hsl = slice(512 * hc + 128 * tb4,
                                            512 * hc + 128 * tb4 + 128)
                                for pi, (ha, woff, hk) in enumerate((
                                        (H1H, 0, f"H1H{hc}"),
                                        (H1H, 4096, f"H1H{hc}"),
                                        (H1L, 0, f"H1L{hc}"))):
                                    for h in range(2):
                                        P.op("tensor",
                                             lambda E, ha=ha, woff=woff,
                                             hsl=hsl, h=h, hc=hc, pst=pst,
                                             st=(not first and hc == 0
                                                 and pi == 0),
                                             sp=(hc == HB - 1 and pi == 2):
                                             E.matmul(
                                                 out=pst[:, 512 * h:
                                                         512 * (h + 1)],
                                                 lhsT=ha[:, hsl],
                                                 rhs=RWQB[:, woff + 1024 * hc
                                                          + 512 * h:
                                                          woff + 1024 * hc
                                                          + 512 * h + 512],
                                                 start=st, stop=sp),
                                             reads=[hk, "RWQB"], writes=pk,
                                             inc="pe" if (hc == HB - 1 and
                                                          pi == 2 and h == 1)
                                             else None)
                            gk = f"G{gtb}"
                            if first:
                                P.op("vector", lambda E, pst=pst, gtb=gtb:
                                     E.tensor_copy(
                                         out=GACC[:, D * gtb:D * (gtb + 1)],
                                         in_=pst[:]),
                                     reads=pk, writes=[gk], inc="dv")
                            elif e == 8:
                                P.op("vector", lambda E, pst=pst, gtb=gtb:
                                     E.tensor_tensor(
                                         out=GACC[:, D * gtb:D * (gtb + 1)],
                                         in0=GACC[:, D * gtb:D * (gtb + 1)],
                                         in1=pst[:], op=ALU.add),
                                     reads=pk + [gk], writes=[gk], inc="dv")
                            else:
                                wcol = 8 * gtb + e
                                P.op("vector", lambda E, pst=pst, gtb=gtb,
                                     wcol=wcol: E.scalar_tensor_tensor(
                                         out=GACC[:, D * gtb:D * (gtb + 1)],
                                         in0=pst[:],
                                         scalar=W8[:, wcol:wcol + 1],
                                         in1=GACC[:, D * gtb:D * (gtb + 1)],
                                         op0=ALU.mult, op1=ALU.add),
                                     reads=pk + [gk, "W8"], writes=[gk],
                                     inc="dv")

            for gtb in range(cfg.tbs):
                P.op("scalar", lambda E, gtb=gtb: E.activation(
                    out=GACC[:, D * gtb:D * (gtb + 1)],
                    in_=GACC[:, D * gtb:D * (gtb + 1)],
                    func=AF.Silu, bias=0.0, scale=1.0),
                    reads=[f"G{gtb}"], writes=[f"G{gtb}"], inc="sc")

            for k in range(NCH):
                dma("sync", RWQA[:, 1024 * k:1024 * (k + 1)], d_wqh[k],
                    reads=["d_wqh"], writes=["RWQA"], wguard=(k == 0))
                dma("sync", RWQB[:, 1024 * k:1024 * (k + 1)], d_wql[k],
                    reads=["d_wql"], writes=["RWQB"], wguard=(k == 0))

            for gtb in range(cfg.tbs):
                t0c = 128 * gtb
                for fo in range(NCH):
                    reg = PS[0][:, 512 * (fo // 4) + 128 * (fo % 4):
                                512 * (fo // 4) + 128 * (fo % 4) + 128]
                    rkey = "PS0A" if fo < 4 else "PS0B"
                    for j in range(NCH):
                        xsl = slice(1024 * j + t0c, 1024 * j + t0c + 128)
                        wsl = slice(D * j + 128 * fo, D * j + 128 * fo + 128)
                        for pi, (wa, xa, wk_, xk) in enumerate((
                                (RWQA, XNH, "RWQA", "XNH"),
                                (RWQA, XNL, "RWQA", "XNL"),
                                (RWQB, XNH, "RWQB", "XNH"))):
                            P.op("tensor", lambda E, wa=wa, xa=xa, wsl=wsl,
                                 xsl=xsl, reg=reg,
                                 st=(j == 0 and pi == 0),
                                 sp=(j == NCH - 1 and pi == 2): E.matmul(
                                     out=reg, lhsT=wa[:, wsl], rhs=xa[:, xsl],
                                     start=st, stop=sp),
                                 reads=[wk_, xk], writes=[rkey],
                                 inc="pe" if (j == NCH - 1 and pi == 2)
                                 else None)
                    b = gtb % 2
                    P.op("scalar", lambda E, reg=reg, fo=fo, b=b:
                         E.activation(out=TMQF[b][:], in_=reg,
                                      func=AF.Identity,
                                      bias=BQ8[:, fo:fo + 1], scale=1.0),
                         reads=[rkey, "BQ8"], writes=[f"TMQF{b}"], inc="sc")
                    P.op("vector", lambda E, fo=fo, b=b: E.tensor_copy(
                        out=MQHB[:, 128 * fo:128 * (fo + 1)], in_=TMQF[b][:]),
                        reads=[f"TMQF{b}"], writes=["MQHB"], inc="dv")
                    P.drain("vector")
                    P.op("vector", lambda E, fo=fo, b=b: E.tensor_tensor(
                        out=MQLB[:, 128 * fo:128 * (fo + 1)], in0=TMQF[b][:],
                        in1=MQHB[:, 128 * fo:128 * (fo + 1)], op=ALU.subtract),
                        reads=[f"TMQF{b}", "MQHB"], writes=["MQLB"])
                    P.op("vector", lambda E, fo=fo, b=b: E.tensor_scalar(
                        out=WVH[:, 128 * fo:128 * (fo + 1)], in0=TMQF[b][:],
                        scalar1=PK8[:, fo:fo + 1], scalar2=None, op0=ALU.mult),
                        reads=[f"TMQF{b}", "PK8"], writes=["WVH"], inc="dv")
                    P.drain("vector")
                    P.op("vector", lambda E, fo=fo, b=b:
                         E.scalar_tensor_tensor(
                             out=WVL[:, 128 * fo:128 * (fo + 1)],
                             in0=TMQF[b][:], scalar=PK8[:, fo:fo + 1],
                             in1=WVH[:, 128 * fo:128 * (fo + 1)],
                             op0=ALU.mult, op1=ALU.subtract),
                         reads=[f"TMQF{b}", "PK8", "WVH"], writes=["WVL"],
                         inc="dv")
                pst = PS[1 + (gtb % 2)]
                pk = [f"PS{1 + (gtb % 2)}A", f"PS{1 + (gtb % 2)}B"]
                for pi, (bt_, bk_) in enumerate(((BTROWH, "BTROWH"),
                                                 (BTROWL, "BTROWL"))):
                    for h in range(2):
                        P.op("tensor", lambda E, bt_=bt_, h=h, pst=pst,
                             st=(pi == 0): E.matmul(
                                 out=pst[:, 512 * h:512 * (h + 1)],
                                 lhsT=ONESH[:],
                                 rhs=bt_[0:1, 512 * h:512 * (h + 1)],
                                 start=st, stop=False),
                             reads=["ONESH", bk_], writes=pk)
                for j in range(NCH):
                    vsl = slice(128 * j, 128 * (j + 1))
                    for pi, (va, wa, vk, wk_) in enumerate((
                            (WVH, WTH, "WVH", "WTH"),
                            (WVH, WTL, "WVH", "WTL"),
                            (WVL, WTH, "WVL", "WTH"))):
                        for h in range(2):
                            P.op("tensor", lambda E, va=va, wa=wa, vsl=vsl,
                                 j=j, h=h, pst=pst: E.matmul(
                                     out=pst[:, 512 * h:512 * (h + 1)],
                                     lhsT=va[:, vsl],
                                     rhs=wa[:, D * j + 512 * h:
                                            D * j + 512 * h + 512],
                                     start=False, stop=False),
                                 reads=[vk, wk_], writes=pk)
                for pi, (ma_, mkk) in enumerate(((MQHB, "MQHB"),
                                                 (MQLB, "MQLB"))):
                    for j in range(NCH):
                        is_last = (pi == 1 and (j == 3 or j == NCH - 1))
                        P.op("tensor", lambda E, ma_=ma_, j=j, pst=pst,
                             sp=is_last: E.matmul(
                                 out=pst[:, 128 * j:128 * (j + 1)],
                                 lhsT=ma_[:, 128 * j:128 * (j + 1)],
                                 rhs=IDH[:], start=False, stop=sp),
                             reads=[mkk, "IDH"], writes=pk,
                             inc="pe" if (pi == 1 and j == NCH - 1) else None)
                b = gtb % 2
                dma("sync", XB[b][:], d_xo[q0 + t0c:q0 + t0c + 128, :],
                    writes=[f"XB{b}"])
                P.op("vector", lambda E, pst=pst, gtb=gtb: E.tensor_tensor(
                    out=OUTB[:], in0=pst[:], in1=GACC[:, D * gtb:D * (gtb + 1)],
                    op=ALU.mult),
                    reads=pk + [f"G{gtb}"], writes=["OUTB"], inc="dv")
                P.drain("vector")
                P.op("vector", lambda E, b=b: E.tensor_tensor(
                    out=OUTB[:], in0=OUTB[:], in1=XB[b][:], op=ALU.add),
                    reads=["OUTB", f"XB{b}"], writes=["OUTB"], inc="dv")
                dma("gpsimd", d_out[q0 + t0c:q0 + t0c + 128, :], OUTB[:],
                    reads=["OUTB"], sem="st")

        for eng in ("vector", "scalar", "tensor"):
            P.drain(eng, inc={"vector": "dv", "scalar": "sc",
                              "tensor": None}[eng])
        P._raw("gpsimd", lambda E: E.nop(),
               waits=[("st", P.cnt["st"])], inc=None, k=1)
        if getattr(cfg, "verbose", False):
            print("arena views:", AR.cur, "arena size", AR.nbytes,
                  "sbuf remaining", nc.sbuf_bytes_remaining)
        P.replay(block)
    return nc


def kernel(**inputs):
    cfg = Cfg()
    x = np.ascontiguousarray(np.asarray(inputs["L0_attn_hidden_states"],
                                        dtype=np.float32))
    B, S, _ = x.shape
    w = {k: np.ascontiguousarray(np.asarray(v, dtype=np.float32))
         for k, v in inputs.items() if k != "L0_attn_hidden_states"}
    nc = build_nc(cfg)
    wmap = {
        "wq": w["L0_attn_Wq"], "bq": w["L0_attn_bq"],
        "wqa": w["L0_attn_Wqa"], "bqa": w["L0_attn_bqa"],
        "wk": w["L0_attn_Wk"], "bk": w["L0_attn_bk"],
        "wka": w["L0_attn_Wka"], "bka": w["L0_attn_bka"],
        "wt": w["L0_attn_Wt"], "bt": w["L0_attn_bt"],
        "rww": w["L0_ffn_router_w"], "rwb": w["L0_ffn_router_b"],
        "sw1": w["L0_ffn_sw1"], "sb1": w["L0_ffn_sb1"],
        "sw2": w["L0_ffn_sw2"], "sb2": w["L0_ffn_sb2"],
        "rw1": w["L0_ffn_rw1"], "rb1": w["L0_ffn_rb1"],
        "rw2": w["L0_ffn_rw2"], "rb2": w["L0_ffn_rb2"],
    }
    in_maps = []
    for c in range(8):
        b = c // 4
        r0 = (c % 4) * cfg.OWN
        im = {"xb": x[b], "xo": np.ascontiguousarray(x[b, r0:r0 + cfg.OWN])}
        im.update(wmap)
        in_maps.append(im)
    res = run_bass_kernel_spmd(nc, in_maps, core_ids=list(range(8)))
    out = np.empty((B, S, D), dtype=np.float32)
    for c in range(8):
        b = c // 4
        r0 = (c % 4) * cfg.OWN
        out[b, r0:r0 + cfg.OWN] = res.results[c]["out"]
    return out
